# version 19
# baseline (speedup 1.0000x reference)
import numpy as np
import concourse.bass as bass
import concourse.mybir as mybir
from concourse.bass_utils import run_bass_kernel_spmd

F32 = mybir.dt.float32
BF16 = mybir.dt.bfloat16
AF = mybir.ActivationFunctionType
ALU = mybir.AluOpType
AX = mybir.AxisListType

B, T, D = 4, 4096, 1024
H, HD = 16, 64
DH = 4096
T_OWN = T // 2
NEG = -1.0e30
FFN_TB = 512
NBLK = T_OWN // FFN_TB


class Prog:

    NPOOL = 10

    def __init__(self, nc, sems):
        self.nc = nc
        self.sems = sems
        self.count = {k: 0 for k in sems}
        self.step = {"T": 1, "V": 1, "S": 1}
        self.stream = {"T": "T", "V": "V", "S": "S"}
        for q, st in (("QS", "QS"), ("QA", "S"), ("QG", "QG")):
            for i in range(self.NPOOL):
                self.step[f"{q}{i}"] = 16
                self.stream[f"{q}{i}"] = st
        self.rr = {"QS": 0, "QA": 0, "QG": 0}
        self.last_w = {}
        self.readers = {}
        self.floors = {}
        self.queue = {s: [] for s in ("T", "V", "S", "QS", "QG")}

    def _wait(self, stream, tag, val):
        fl = self.floors.setdefault(stream, {})
        if fl.get(tag, 0) >= val:
            return
        fl[tag] = val
        self.queue[stream].append(("w", tag, val))

    def barrier(self):
        snap = dict(self.count)
        for stream in self.queue:
            for tag, val in snap.items():
                if val and self.stream[tag] != stream:
                    self._wait(stream, tag, val)

    def op(self, tag, fn, reads=(), writes=()):
        stream = self.stream[tag]
        if getattr(self, "serialize_all", False):
            for t2, v2 in self.count.items():
                if v2 and self.stream[t2] != stream:
                    self._wait(stream, t2, v2)
        need = {}
        for k in reads:
            lw = self.last_w.get(k)
            if lw:
                wtag, wval = lw
                if self.stream[wtag] != stream or self.step[wtag] == 16:
                    need[wtag] = max(need.get(wtag, 0), wval)
                elif self.count[wtag] - wval < 8 * self.step[wtag]:
                    need[wtag] = max(need.get(wtag, 0), wval)
        for k in writes:
            lw = self.last_w.get(k)
            if lw:
                wtag, wval = lw
                if self.stream[wtag] != stream or self.step[wtag] == 16:
                    need[wtag] = max(need.get(wtag, 0), wval)
            for rtag, rval in self.readers.get(k, {}).items():
                if self.stream[rtag] != stream or self.step[rtag] == 16:
                    need[rtag] = max(need.get(rtag, 0), rval)
        for t in sorted(need):
            self._wait(stream, t, need[t])
        self.count[tag] += self.step[tag]
        val_after = self.count[tag]
        self.queue[stream].append(("op", tag, fn))
        self.trace_log = getattr(self, "trace_log", [])
        self.trace_log.append((stream, tag, tuple(reads), tuple(writes),
                               dict(need)))
        for k in reads:
            self.readers.setdefault(k, {})[tag] = val_after
        for k in writes:
            self.last_w[k] = (tag, val_after)
            self.readers[k] = {}
        return val_after

    def dma(self, qtag, out, in_, reads=(), writes=()):
        i = self.rr[qtag]
        self.rr[qtag] = (i + 1) % self.NPOOL
        tag = f"{qtag}{i}"
        stream = self.stream[tag]
        self._wait(stream, tag, self.count[tag])
        self.op(tag, lambda e: e.dma_start(out=out, in_=in_), reads, writes)

    def attach(self, block):
        finals = dict(self.count)
        waited = {}
        for q in self.queue.values():
            for kind, tag, x in q:
                if kind == "w":
                    waited.setdefault(tag, set()).add(x)
        for t, v in finals.items():
            if v:
                waited.setdefault(t, set()).add(v)
        cum = {t: 0 for t in self.count}
        pend = {t: 0 for t in self.count}
        incs = {s: [] for s in self.queue}
        for s, q in self.queue.items():
            pass
        for s, q in self.queue.items():
            for kind, tag, x in q:
                if kind != "op":
                    continue
        for s, q in self.queue.items():
            lst = []
            for kind, tag, x in q:
                if kind == "w":
                    lst.append(None)
                    continue
                cum[tag] += self.step[tag]
                pend[tag] += self.step[tag]
                if cum[tag] in waited.get(tag, ()):
                    lst.append((tag, pend[tag]))
                    pend[tag] = 0
                else:
                    lst.append((tag, 0))
            incs[s] = lst

        def mk(stream, final_wait=False):
            def body(e):
                for (kind, tag, x), inc in zip(self.queue[stream],
                                               incs[stream]):
                    if kind == "w":
                        e.wait_ge(self.sems[tag], x)
                    else:
                        ins = x(e)
                        if inc and inc[1]:
                            ins.then_inc(self.sems[inc[0]], inc[1])
                if final_wait:
                    for t in sorted(finals):
                        if finals[t]:
                            e.wait_ge(self.sems[t], finals[t])
            return body
        block.tensor(mk("T"))
        block.vector(mk("V"))
        block.scalar(mk("S"))
        block.gpsimd(mk("QG"))
        block.sync(mk("QS", final_wait=True))


def build_core(dump=False, serialize=False):
    nc = bass.Bass("TRN2", target_bir_lowering=False, debug=False)

    def inp(name, shape):
        return nc.dram_tensor(name, shape, F32, kind="ExternalInput").ap()

    xb = inp("xb", [T, D])
    xq = inp("xq", [T_OWN, D])
    WkT = inp("WkT", [D, D])
    WqT = inp("WqT", [D, D])
    WvT = inp("WvT", [D, D])
    WoT = inp("WoT", [D, D])
    w1T = inp("w1T", [D, DH])
    w2T = inp("w2T", [D, DH])
    h1T = inp("h1T", [DH, DH])
    h2T = inp("h2T", [DH, DH])
    w3T = inp("w3T", [DH, D])
    s1 = inp("s1", [128, 32])
    s2 = inp("s2", [128, 32])
    sh1 = inp("sh1", [128, 32])
    sh2 = inp("sh2", [128, 32])
    s3 = inp("s3", [128, 8])
    Cal = inp("Cal", [128, T])
    Sal = inp("Sal", [128, T])
    Cow = inp("Cow", [128, T_OWN])
    Sow = inp("Sow", [128, T_OWN])
    mask_e = inp("mask_e", [128, 128])
    mask_o = inp("mask_o", [128, 128])
    swp = inp("swp", [128, 128])
    ident = inp("ident", [128, 128])
    ones64 = inp("ones64", [1, 64])

    y = nc.dram_tensor("y", [T_OWN, D], F32, kind="ExternalOutput").ap()

    scr_kind = "ExternalOutput" if dump else "Internal"
    K_scr_hi = nc.dram_tensor("K_scr_hi", [D, T], BF16, kind=scr_kind).ap()
    K_scr_lo = nc.dram_tensor("K_scr_lo", [D, T], BF16, kind=scr_kind).ap()
    Q_scr_hi = nc.dram_tensor("Q_scr_hi", [D, T_OWN], BF16, kind=scr_kind).ap()
    Q_scr_lo = nc.dram_tensor("Q_scr_lo", [D, T_OWN], BF16, kind=scr_kind).ap()
    V_scr_hi = nc.dram_tensor("V_scr_hi", [T, D], BF16, kind=scr_kind).ap()
    V_scr_lo = nc.dram_tensor("V_scr_lo", [T, D], BF16, kind=scr_kind).ap()
    xnT_scr_hi = nc.dram_tensor("xnT_scr_hi", [D, T_OWN], BF16, kind=scr_kind).ap()
    xnT_scr_lo = nc.dram_tensor("xnT_scr_lo", [D, T_OWN], BF16, kind=scr_kind).ap()
    attn_scr = nc.dram_tensor("attn_scr", [D, T_OWN], F32, kind=scr_kind).ap()
    w1b = nc.dram_tensor("w1b", [D, DH], BF16, kind="Internal").ap()
    w2b = nc.dram_tensor("w2b", [D, DH], BF16, kind="Internal").ap()
    h1b = nc.dram_tensor("h1b", [DH, DH], BF16, kind="Internal").ap()
    h2b = nc.dram_tensor("h2b", [DH, DH], BF16, kind="Internal").ap()
    w3b = nc.dram_tensor("w3b", [DH, D], BF16, kind="Internal").ap()
    O_scr_hi = nc.dram_tensor("O_scr_hi", [D, T_OWN], BF16, kind="Internal").ap()
    O_scr_lo = nc.dram_tensor("O_scr_lo", [D, T_OWN], BF16, kind="Internal").ap()
    st_dbg = nc.dram_tensor("st_dbg", [128, 16], F32, kind=scr_kind).ap()
    st_steps = [nc.dram_tensor(f"st_step{i}", [128, 16], F32, kind=scr_kind).ap()
                for i in range(6)]
    h_dbg = nc.dram_tensor("h_dbg", [128, 32 * 512], F32, kind=scr_kind).ap()
    g1_dbg = nc.dram_tensor("g1_dbg", [128, 32 * 512], F32, kind=scr_kind).ap()
    g2_dbg = nc.dram_tensor("g2_dbg", [128, 32 * 512], F32, kind=scr_kind).ap()
    xn_dbg = nc.dram_tensor("xn_dbg", [128, 4096], F32, kind=scr_kind).ap()
    wk_dbg = nc.dram_tensor("wk_dbg", [128, 8192], BF16, kind=scr_kind).ap()
    kraw_dbg = nc.dram_tensor("kraw_dbg", [128, 512], F32, kind=scr_kind).ap()
    wkl2_dbg = nc.dram_tensor("wkl2_dbg", [128, 8192], BF16, kind=scr_kind).ap()
    wkh2_dbg = nc.dram_tensor("wkh2_dbg", [128, 8192], BF16, kind=scr_kind).ap()
    xth_dbg = nc.dram_tensor("xth_dbg", [128, 4096], BF16, kind=scr_kind).ap()
    xtl_dbg = nc.dram_tensor("xtl_dbg", [128, 4096], BF16, kind=scr_kind).ap()
    wkl_dbg = nc.dram_tensor("wkl_dbg", [128, 8192], BF16, kind=scr_kind).ap()

    import contextlib
    with contextlib.ExitStack() as top:
        sems = {}
        for t in ("T", "V", "S"):
            sems[t] = top.enter_context(nc.semaphore(f"sm{t}"))
        for q in ("QS", "QA", "QG"):
            for i in range(Prog.NPOOL):
                sems[f"{q}{i}"] = top.enter_context(
                    nc.semaphore(f"sm{q}{i}"))
        block = top.enter_context(nc.Block())
        P = Prog(nc, sems)
        P.serialize_all = serialize
        globals()['LAST_PROG'] = P

        def sb(stack, name, p, f, dt=F32):
            return stack.enter_context(nc.sbuf_tensor(name, [p, f], dt))

        g_sc = top.enter_context(contextlib.ExitStack())
        swp_sb = sb(g_sc, "swp_sb", 128, 128)
        id_sb = sb(g_sc, "id_sb", 128, 128)
        me_sb = sb(g_sc, "me_sb", 128, 128)
        mo_sb = sb(g_sc, "mo_sb", 128, 128)
        on_sb = sb(g_sc, "on_sb", 1, 64)
        nb_sb = sb(g_sc, "nb_sb", 128, 1)
        P.dma("QS", swp_sb[:], swp, writes=["swp"])
        P.dma("QS", id_sb[:], ident, writes=["id"])
        P.dma("QS", me_sb[:], mask_e, writes=["me"])
        P.dma("QS", mo_sb[:], mask_o, writes=["mo"])
        P.dma("QS", on_sb[:], ones64, writes=["on"])
        P.op("V", lambda e: e.memset(nb_sb[:], -30.0), [], ["nb"])

        pt = [top.enter_context(nc.psum_tensor(f"pt{i}", [128, 512], F32))
              for i in range(8)]
        for wsrc, wdst, kk in ((w1T, w1b, "w1b"), (w2T, w2b, "w2b"),
                               (h1T, h1b, "h1b"), (h2T, h2b, "h2b"),
                               (w3T, w3b, "w3b")):
            P.dma("QG", wdst, wsrc, writes=[kk])
        pk = [f"pt{i}" for i in range(8)]

        dump_hooks = {"n": 0, "on": False}

        def stdump(st_sb, stkey):
            if not dump_hooks["on"]:
                return
            i = dump_hooks["n"]
            if i >= len(st_steps):
                return
            dump_hooks["n"] = i + 1
            P.dma("QA", st_steps[i], st_sb[:], reads=[stkey],
                  writes=[f"stddbg{i}"])

        def layernorm(x_sb, xkey, nt, st_sb, stkey, xn_sb, xnkey):
            for j in range(nt):
                P.op("V", lambda e, j=j: e.reduce_sum(
                    st_sb[:, j:j + 1], x_sb[:, 1024 * j:1024 * (j + 1)],
                    axis=AX.X), [xkey], [stkey])
            P.op("V", lambda e: e.tensor_scalar(
                st_sb[:, 0:nt], st_sb[:, 0:nt], 1.0 / D, None, ALU.mult),
                [stkey], [stkey])
            stdump(st_sb, stkey)
            for j in range(nt):
                P.op("S", lambda e, j=j: e.activation(
                    xn_sb[:, 1024 * j:1024 * (j + 1)],
                    x_sb[:, 1024 * j:1024 * (j + 1)], AF.Square,
                    accum_out=st_sb[:, nt + j:nt + j + 1]),
                    [xkey], [xnkey, stkey])
            stdump(st_sb, stkey)
            P.op("V", lambda e: e.tensor_mul(
                st_sb[:, 2 * nt:3 * nt], st_sb[:, 0:nt], st_sb[:, 0:nt]),
                [stkey], [stkey])
            stdump(st_sb, stkey)
            P.op("V", lambda e: e.tensor_scalar(
                st_sb[:, nt:2 * nt], st_sb[:, nt:2 * nt], 1.0 / D, None,
                ALU.mult), [stkey], [stkey])
            P.op("V", lambda e: e.tensor_sub(
                st_sb[:, nt:2 * nt], st_sb[:, nt:2 * nt],
                st_sb[:, 2 * nt:3 * nt]), [stkey], [stkey])
            stdump(st_sb, stkey)
            P.op("V", lambda e: e.tensor_scalar(
                st_sb[:, nt:2 * nt], st_sb[:, nt:2 * nt], 1e-5, None,
                ALU.add), [stkey], [stkey])
            P.op("S", lambda e: e.activation(
                st_sb[:, 2 * nt:3 * nt], st_sb[:, nt:2 * nt], AF.Sqrt),
                [stkey], [stkey])
            P.op("V", lambda e: e.reciprocal(
                st_sb[:, 2 * nt:3 * nt], st_sb[:, 2 * nt:3 * nt]),
                [stkey], [stkey])
            P.op("V", lambda e: e.scalar_tensor_tensor(
                st_sb[:, 3 * nt:4 * nt], st_sb[:, 0:nt], -1.0,
                st_sb[:, 2 * nt:3 * nt], ALU.mult, ALU.mult),
                [stkey], [stkey])
            stdump(st_sb, stkey)
            for j in range(nt):
                P.op("S", lambda e, j=j: e.activation(
                    xn_sb[:, 1024 * j:1024 * (j + 1)],
                    x_sb[:, 1024 * j:1024 * (j + 1)], AF.Identity,
                    bias=st_sb[:, 3 * nt + j:3 * nt + j + 1],
                    scale=st_sb[:, 2 * nt + j:2 * nt + j + 1]),
                    [xkey, stkey], [xnkey])

        def transpose_tiles(xn_sb, xnkey, xnT_sb, xnTkey, nt, pbanks):
            w = 128 * nt
            for c in range(8):
                for j in range(nt):
                    bi = (c * nt + j) % 2
                    pb, pbk = pbanks[bi], pbanks[bi + 2]
                    P.op("T", lambda e, c=c, j=j, pb=pb: e.transpose(
                        pb[:, 0:128], xn_sb[:, 1024 * j + 128 * c:
                                            1024 * j + 128 * (c + 1)], id_sb[:]),
                        [xnkey, "id"], [pbk])
                    P.op("V", lambda e, c=c, j=j, pb=pb: e.tensor_copy(
                        xnT_sb[:, w * c + 128 * j: w * c + 128 * (j + 1)],
                        pb[:, 0:128]), [pbk], [xnTkey])

        def prep_w_hilo(wsrc, whi, wlo, khi, klo, tmp_sb, ktmp, nic=8):
            P.dma("QG", whi[:].rearrange("p (ic o) -> p ic o", o=1024),
                  wsrc.rearrange("(ic p) o -> p ic o", p=128), writes=[khi])
            for ic in range(nic):
                P.dma("QS", tmp_sb[:, 0:1024],
                      wsrc[128 * ic:128 * (ic + 1), :],
                      writes=[ktmp])
                P.op("V", lambda e, ic=ic: e.tensor_sub(
                    wlo[:, 1024 * ic:1024 * (ic + 1)], tmp_sb[:, 0:1024],
                    whi[:, 1024 * ic:1024 * (ic + 1)]),
                    [ktmp, khi], [klo])

        import os
        PROJMODE = os.environ.get("PROJMODE", "3")

        def proj3(pb, pbk, whi, wlo, wkh, wkl, xth, xtl, kxh, kxl, oo):
            if PROJMODE == "1":
                for ic in range(8):
                    sl = slice(1024 * ic + 128 * oo,
                               1024 * ic + 128 * (oo + 1))
                    xsl = slice(512 * ic, 512 * (ic + 1))
                    P.op("T", lambda e, sl=sl, xsl=xsl, pb=pb, whi=whi,
                         xth=xth, ic=ic: e.matmul(
                             pb[:, :], whi[:, sl], xth[:, xsl],
                             start=(ic == 0), stop=(ic == 7)),
                         [wkh, kxh], [pbk])
                return
            for ic in range(8):
                sl = slice(1024 * ic + 128 * oo, 1024 * ic + 128 * (oo + 1))
                xsl = slice(512 * ic, 512 * (ic + 1))
                P.op("T", lambda e, sl=sl, xsl=xsl, pb=pb, whi=whi, xth=xth,
                     ic=ic: e.matmul(pb[:, :], whi[:, sl], xth[:, xsl],
                                     start=(ic == 0), stop=False),
                     [wkh, kxh], [pbk])
                P.op("T", lambda e, sl=sl, xsl=xsl, pb=pb, whi=whi, xtl=xtl:
                     e.matmul(pb[:, :], whi[:, sl], xtl[:, xsl],
                              start=False, stop=False),
                     [wkh, kxl], [pbk])
                P.op("T", lambda e, sl=sl, xsl=xsl, pb=pb, wlo=wlo, xth=xth,
                     ic=ic: e.matmul(pb[:, :], wlo[:, sl], xth[:, xsl],
                                     start=False, stop=(ic == 7)),
                     [wkl, kxh], [pbk])

        def transpose_split(xn_sb, xnkey, xth, xtl, kh, kl, nt, pbanks):
            w = 128 * nt
            for c in range(8):
                for j in range(nt):
                    bi = (c * nt + j) % 2
                    pb, pbk = pbanks[bi], pbanks[bi + 2]
                    P.op("T", lambda e, c=c, j=j, pb=pb: e.transpose(
                        pb[:, 0:128], xn_sb[:, 1024 * j + 128 * c:
                                            1024 * j + 128 * (c + 1)],
                        id_sb[:]), [xnkey, "id"], [pbk])
                    sl = slice(w * c + 128 * j, w * c + 128 * (j + 1))
                    P.op("V", lambda e, sl=sl, pb=pb: e.tensor_copy(
                        xth[:, sl], pb[:, 0:128]), [pbk], [kh])
                    P.op("V", lambda e, sl=sl, pb=pb: e.tensor_sub(
                        xtl[:, sl], pb[:, 0:128], xth[:, sl]),
                        [pbk, kh], [kl])

        def split_to(dst_hi, dst_lo, kh, kl, src, ks, sl):
            P.op("V", lambda e: e.tensor_copy(dst_hi[:, sl], src), [ks], [kh])
            P.op("V", lambda e: e.tensor_sub(dst_lo[:, sl], src,
                                             dst_hi[:, sl]), [ks, kh], [kl])

        with contextlib.ExitStack() as sc:
            wk_hi = sb(sc, "wk_hi", 128, 8 * 1024, BF16)
            wk_lo = sb(sc, "wk_lo", 128, 8 * 1024, BF16)
            wv_hi = sb(sc, "wv_hi", 128, 8 * 1024, BF16)
            wv_lo = sb(sc, "wv_lo", 128, 8 * 1024, BF16)
            cal_sb2 = [sb(sc, f"cal_sb{i}", 128, 512) for i in range(2)]
            sal_sb2 = [sb(sc, f"sal_sb{i}", 128, 512) for i in range(2)]
            x_sb2 = [sb(sc, f"x_sb{i}", 128, 4 * 1024) for i in range(2)]
            xn_sb2 = [sb(sc, f"xn_sb{i}", 128, 4 * 1024) for i in range(2)]
            x_sb = x_sb2[0]
            xth_sb = [sb(sc, f"xth{i}", 128, 8 * 512, BF16) for i in range(2)]
            xtl_sb = [sb(sc, f"xtl{i}", 128, 8 * 512, BF16) for i in range(2)]
            st_sb2 = [sb(sc, f"st_sb{i}", 128, 16) for i in range(2)]
            kraw = [sb(sc, f"kraw{i}", 128, 512) for i in range(2)]
            t1_sb = [sb(sc, f"t1_{i}", 128, 512) for i in range(2)]
            rot_sb = [sb(sc, f"rot{i}", 128, 512) for i in range(2)]
            rh_sb = [sb(sc, f"rh{i}", 128, 512, BF16) for i in range(2)]
            rl_sb = [sb(sc, f"rl{i}", 128, 512, BF16) for i in range(2)]
            vt_sb = [sb(sc, f"vt{i}", 128, 512) for i in range(2)]
            vth_sb = [sb(sc, f"vth{i}", 128, 512, BF16) for i in range(2)]
            vtl_sb = [sb(sc, f"vtl{i}", 128, 512, BF16) for i in range(2)]

            prep_w_hilo(WkT, wk_hi, wk_lo, "wkh", "wkl", x_sb2[0], "x00")
            if dump:
                P.dma("QA", wk_dbg, wk_hi[:], reads=["wkh"], writes=["wkd"])
                P.dma("QA", wkl_dbg, wk_lo[:], reads=["wkl"], writes=["wkld"])
            prep_w_hilo(WvT, wv_hi, wv_lo, "wvh", "wvl", x_sb2[0], "x00")


            for g in range(8):
                x_sb, xn_sb = x_sb2[g % 2], xn_sb2[g % 2]
                kx, kxn = f"x0{g % 2}", f"xn0{g % 2}"
                cal_sb, sal_sb = cal_sb2[g % 2], sal_sb2[g % 2]
                kcal, ksal = f"cal{g % 2}", f"sal{g % 2}"
                P.dma("QS", cal_sb[:], Cal[:, 512 * g:512 * (g + 1)],
                      writes=[kcal])
                P.dma("QS", sal_sb[:], Sal[:, 512 * g:512 * (g + 1)],
                      writes=[ksal])
                P.dma("QS", x_sb[:].rearrange("p (j d) -> p j d", d=1024),
                      xb[512 * g:512 * (g + 1), :].rearrange(
                          "(j p) d -> p j d", p=128),
                      writes=[kx])
                layernorm(x_sb, kx, 4, st_sb2[g % 2], f"st0{g % 2}", xn_sb, kxn)
                xth, xtl = xth_sb[g % 2], xtl_sb[g % 2]
                kxh, kxl = f"xth{g % 2}", f"xtl{g % 2}"
                transpose_split(xn_sb, kxn, xth, xtl, kxh, kxl, 4,
                                (pt[6], pt[7], pk[6], pk[7]))
                def k_tail(oc, g=g, cal_sb=cal_sb, sal_sb=sal_sb,
                           kcal=kcal, ksal=ksal):
                    oo = oc % 8
                    pb, pbk = pt[oc % 2], pk[oc % 2]
                    kr = kraw[oc % 2]
                    krk = f"kraw{oc % 2}"
                    P.op("V", lambda e, pb=pb, kr=kr: e.tensor_copy(
                        kr[:], pb[:, :]), [pbk], [krk])
                    psw = pt[2 + oc % 2]
                    pswk = pk[2 + oc % 2]
                    P.op("T", lambda e, kr=kr, psw=psw: e.matmul(
                        psw[:, :], swp_sb[:], kr[:], start=True,
                        stop=True), [krk, "swp"], [pswk])
                    tl = t1_sb[oc % 2]
                    tlk = f"t1_{oc % 2}"
                    ro = rot_sb[oc % 2]
                    rok = f"rot{oc % 2}"
                    P.op("V", lambda e, kr=kr, tl=tl, cal_sb=cal_sb:
                         e.tensor_mul(tl[:], kr[:], cal_sb[:]),
                         [krk, kcal], [tlk])
                    P.op("V", lambda e, psw=psw, ro=ro, sal_sb=sal_sb:
                         e.tensor_mul(ro[:], psw[:, :], sal_sb[:]),
                         [pswk, ksal], [rok])
                    P.op("V", lambda e, tl=tl, ro=ro: e.tensor_add(
                        ro[:], ro[:], tl[:]), [tlk, rok], [rok])
                    rh, rl = rh_sb[oc % 2], rl_sb[oc % 2]
                    rhk, rlk = f"rh{oc % 2}", f"rl{oc % 2}"
                    P.op("V", lambda e, ro=ro, rh=rh: e.tensor_copy(
                        rh[:], ro[:]), [rok], [rhk])
                    P.op("V", lambda e, ro=ro, rh=rh, rl=rl: e.tensor_sub(
                        rl[:], ro[:], rh[:]), [rok, rhk], [rlk])
                    P.dma("QA", K_scr_hi[128 * oo:128 * (oo + 1),
                                         512 * g:512 * (g + 1)], rh[:],
                          reads=[rhk], writes=["K_scr"])
                    P.dma("QA", K_scr_lo[128 * oo:128 * (oo + 1),
                                         512 * g:512 * (g + 1)], rl[:],
                          reads=[rlk], writes=["K_scr"])

                def v_tail(oc, g=g):
                    oo = oc % 8
                    pb, pbk = pt[oc % 2], pk[oc % 2]
                    vr = kraw[oc % 2]
                    vrk = f"kraw{oc % 2}"
                    P.op("V", lambda e, pb=pb, vr=vr: e.tensor_copy(
                        vr[:], pb[:, :]), [pbk], [vrk])
                    vt = vt_sb[oc % 2]
                    vtk = f"vt{oc % 2}"
                    for j in range(4):
                        pb2 = pt[4 + j % 2]
                        pb2k = pk[4 + j % 2]
                        P.op("T", lambda e, vr=vr, j=j, pb2=pb2:
                             e.transpose(pb2[:, 0:128],
                                         vr[:, 128 * j:128 * (j + 1)],
                                         id_sb[:]),
                             [vrk, "id"], [pb2k])
                        P.op("V", lambda e, pb2=pb2, j=j, vt=vt:
                             e.tensor_copy(vt[:, 128 * j:128 * (j + 1)],
                                           pb2[:, 0:128]), [pb2k], [vtk])
                    vh, vl = vth_sb[oc % 2], vtl_sb[oc % 2]
                    vhk, vlk = f"vth{oc % 2}", f"vtl{oc % 2}"
                    P.op("V", lambda e, vt=vt, vh=vh: e.tensor_copy(
                        vh[:], vt[:]), [vtk], [vhk])
                    P.op("V", lambda e, vt=vt, vh=vh, vl=vl: e.tensor_sub(
                        vl[:], vt[:], vh[:]), [vtk, vhk], [vlk])
                    for scr, vv, kk in ((V_scr_hi, vh, vhk),
                                        (V_scr_lo, vl, vlk)):
                        P.dma("QG",
                              scr[512 * g:512 * (g + 1),
                                  128 * oo:128 * (oo + 1)].rearrange(
                                  "(j p) c -> p j c", p=128),
                              vv[:].rearrange("p (j c) -> p j c", c=128),
                              reads=[kk], writes=["V_scr"])

                for oc in range(17):
                    if oc < 16:
                        pb = pt[oc % 2]
                        pbk = pk[oc % 2]
                        if oc < 8:
                            proj3(pb, pbk, wk_hi, wk_lo, "wkh", "wkl",
                                  xth, xtl, kxh, kxl, oc)
                        else:
                            proj3(pb, pbk, wv_hi, wv_lo, "wvh", "wvl",
                                  xth, xtl, kxh, kxl, oc - 8)
                    if oc > 0:
                        if oc - 1 < 8:
                            k_tail(oc - 1)
                        else:
                            v_tail(oc - 1)

            wq_hi, wq_lo = wk_hi, wk_lo
            cow_sb2 = cal_sb2
            sow_sb2 = sal_sb2
            prep_w_hilo(WqT, wq_hi, wq_lo, "wkh", "wkl", x_sb2[0], "x00")


            for g in range(4):
                x_sb, xn_sb = x_sb2[g % 2], xn_sb2[g % 2]
                kx, kxn = f"x0{g % 2}", f"xn0{g % 2}"
                cow_sb, sow_sb = cow_sb2[g % 2], sow_sb2[g % 2]
                kcow, ksow = f"cal{g % 2}", f"sal{g % 2}"
                P.dma("QS", cow_sb[:], Cow[:, 512 * g:512 * (g + 1)],
                      writes=[kcow])
                P.dma("QS", sow_sb[:], Sow[:, 512 * g:512 * (g + 1)],
                      writes=[ksow])
                P.dma("QS", x_sb[:].rearrange("p (j d) -> p j d", d=1024),
                      xq[512 * g:512 * (g + 1), :].rearrange(
                          "(j p) d -> p j d", p=128),
                      writes=[kx])
                dump_hooks["on"] = dump and g == 0
                layernorm(x_sb, kx, 4, st_sb2[g % 2], f"st0{g % 2}", xn_sb, kxn)
                dump_hooks["on"] = False
                xth, xtl = xth_sb[g % 2], xtl_sb[g % 2]
                kxh, kxl = f"xth{g % 2}", f"xtl{g % 2}"
                transpose_split(xn_sb, kxn, xth, xtl, kxh, kxl, 4,
                                (pt[6], pt[7], pk[6], pk[7]))
                for c in range(8):
                    P.dma("QA", xnT_scr_hi[128 * c:128 * (c + 1),
                                           512 * g:512 * (g + 1)],
                          xth[:, 512 * c:512 * (c + 1)],
                          reads=[kxh], writes=["xnT_scr"])
                    P.dma("QA", xnT_scr_lo[128 * c:128 * (c + 1),
                                           512 * g:512 * (g + 1)],
                          xtl[:, 512 * c:512 * (c + 1)],
                          reads=[kxl], writes=["xnT_scr"])
                def q_tail(oc, g=g, cow_sb=cow_sb, sow_sb=sow_sb,
                           kcow=kcow, ksow=ksow):
                    pb, pbk = pt[oc % 2], pk[oc % 2]
                    qr = kraw[oc % 2]
                    qrk = f"kraw{oc % 2}"
                    P.op("V", lambda e, pb=pb, qr=qr: e.tensor_copy(
                        qr[:], pb[:, :]), [pbk], [qrk])
                    psw = pt[2 + oc % 2]
                    pswk = pk[2 + oc % 2]
                    P.op("T", lambda e, qr=qr, psw=psw: e.matmul(
                        psw[:, :], swp_sb[:], qr[:], start=True, stop=True),
                        [qrk, "swp"], [pswk])
                    tl = t1_sb[oc % 2]
                    tlk = f"t1_{oc % 2}"
                    ro = rot_sb[oc % 2]
                    rok = f"rot{oc % 2}"
                    P.op("V", lambda e, qr=qr, tl=tl, cow_sb=cow_sb:
                         e.tensor_mul(tl[:], qr[:], cow_sb[:]),
                         [qrk, kcow], [tlk])
                    P.op("V", lambda e, psw=psw, ro=ro, sow_sb=sow_sb:
                         e.tensor_mul(ro[:], psw[:, :], sow_sb[:]),
                         [pswk, ksow], [rok])
                    P.op("V", lambda e, tl=tl, ro=ro: e.tensor_add(
                        ro[:], ro[:], tl[:]), [tlk, rok], [rok])
                    rh, rl = rh_sb[oc % 2], rl_sb[oc % 2]
                    rhk, rlk = f"rh{oc % 2}", f"rl{oc % 2}"
                    P.op("V", lambda e, ro=ro, rh=rh: e.tensor_copy(
                        rh[:], ro[:]), [rok], [rhk])
                    P.op("V", lambda e, ro=ro, rh=rh, rl=rl: e.tensor_sub(
                        rl[:], ro[:], rh[:]), [rok, rhk], [rlk])
                    P.dma("QA", Q_scr_hi[128 * oc:128 * (oc + 1),
                                         512 * g:512 * (g + 1)], rh[:],
                          reads=[rhk], writes=["Q_scr"])
                    P.dma("QA", Q_scr_lo[128 * oc:128 * (oc + 1),
                                         512 * g:512 * (g + 1)], rl[:],
                          reads=[rlk], writes=["Q_scr"])

                for oc in range(9):
                    if oc < 8:
                        pb = pt[oc % 2]
                        pbk = pk[oc % 2]
                        proj3(pb, pbk, wq_hi, wq_lo, "wkh", "wkl",
                              xth, xtl, kxh, kxl, oc)
                    if oc > 0:
                        q_tail(oc - 1)

        P.barrier()
        with contextlib.ExitStack() as sc:
            kgh_sb2 = [sb(sc, f"kgh_sb{i}", 128, 2 * T, BF16) for i in range(2)]
            kgl_sb2 = [sb(sc, f"kgl_sb{i}", 128, 2 * T, BF16) for i in range(2)]
            vgh_sb2 = [sb(sc, f"vgh_sb{i}", 128, 32 * 260, BF16) for i in range(2)]
            vgl_sb2 = [sb(sc, f"vgl_sb{i}", 128, 32 * 260, BF16) for i in range(2)]
            qgh_sb = [sb(sc, f"qgh{i}", 128, 2 * 512, BF16) for i in range(2)]
            qgl_sb = [sb(sc, f"qgl{i}", 128, 2 * 512, BF16) for i in range(2)]
            ef_sb = [sb(sc, f"ef{i}", 128, 512) for i in range(8)]
            eh_sb = [sb(sc, f"eh{i}", 128, 512, BF16) for i in range(8)]
            el_sb = [sb(sc, f"el{i}", 128, 512, BF16) for i in range(8)]
            ocp_sb = [sb(sc, f"ocp{i}", 128, 512) for i in range(4)]
            rs_sb2 = [sb(sc, f"rs_sb{i}", 1, 512) for i in range(2)]
            rep_sb2 = [sb(sc, f"rep_sb{i}", 128, 512) for i in range(2)]
            onorm_sb2 = [sb(sc, f"onorm_sb{i}", 128, 512) for i in range(2)]
            o_hi = sb(sc, "o_hi", 128, 2 * T_OWN, BF16)
            o_lo = sb(sc, "o_lo", 128, 2 * T_OWN, BF16)

            def load_grp(grp):
                par = grp % 2
                kgh, kgl = kgh_sb2[par], kgl_sb2[par]
                vgh, vgl = vgh_sb2[par], vgl_sb2[par]
                kkh, kkl = f"kgh{par}", f"kgl{par}"
                vkh, vkl = f"vgh{par}", f"vgl{par}"
                for scr, dst, kk in ((K_scr_hi, kgh, kkh),
                                     (K_scr_lo, kgl, kkl)):
                    P.dma("QG", dst[:].rearrange("p (c t) -> p c t", t=T),
                          scr[256 * grp:256 * (grp + 1), :].rearrange(
                              "(c p) t -> p c t", p=128),
                          reads=["K_scr"], writes=[kk])
                P.op("V", lambda e, vgh=vgh: e.memset(
                    vgh[:].rearrange("p (j h u) -> p j h u", h=4, u=65)
                    [:, :, :, 64:65], 1.0), [], [vkh])
                P.op("V", lambda e, vgl=vgl: e.memset(
                    vgl[:].rearrange("p (j h u) -> p j h u", h=4, u=65)
                    [:, :, :, 64:65], 0.0), [], [vkl])
                for scr, dst, kk in ((V_scr_hi, vgh, vkh),
                                     (V_scr_lo, vgl, vkl)):
                    for j in range(32):
                        P.dma("QG",
                              dst[:, 260 * j:260 * (j + 1)].rearrange(
                                  "p (h u) -> p h u", u=65)[:, :, 0:64],
                              scr[128 * j:128 * (j + 1),
                                  256 * grp:256 * (grp + 1)].rearrange(
                                  "p (h d) -> p h d", d=64),
                              reads=["V_scr"], writes=[kk])

            load_grp(0)
            pending_norm = []
            for grp in range(4):
                par = grp % 2
                kgh_sb, kgl_sb = kgh_sb2[par], kgl_sb2[par]
                vgh_sb, vgl_sb = vgh_sb2[par], vgl_sb2[par]
                kkgh, kkgl = f"kgh{par}", f"kgl{par}"
                kvgh, kvgl = f"vgh{par}", f"vgl{par}"
                if grp < 3:
                    load_grp(grp + 1)
                def norm_rest(gp, grp=grp):
                    for h in range(4):
                        ocp = ocp_sb[h]
                        ock = f"ocp{h}"
                        rs_sb, rsk = rs_sb2[h % 2], f"rs{h % 2}"
                        rep_sb, repk = rep_sb2[h % 2], f"rep{h % 2}"
                        onorm_sb, onk = onorm_sb2[h % 2], f"onorm{h % 2}"
                        P.op("V", lambda e, ocp=ocp, rs_sb=rs_sb:
                             e.tensor_scalar(rs_sb[0:1, :], ocp[64:65, :],
                                             1e-30, None, ALU.max),
                             [ock], [rsk])
                        P.op("V", lambda e, rs_sb=rs_sb: e.reciprocal(
                            rs_sb[0:1, :], rs_sb[0:1, :]), [rsk], [rsk])
                        pb2 = pt[h]
                        pb2k = pk[h]
                        P.op("T", lambda e, pb2=pb2, rs_sb=rs_sb: e.matmul(
                            pb2[0:64, :], on_sb[0:1, 0:64], rs_sb[0:1, :],
                            start=True, stop=True), [rsk, "on"], [pb2k])
                        P.op("V", lambda e, pb2=pb2, rep_sb=rep_sb:
                             e.tensor_copy(rep_sb[0:64, :], pb2[0:64, :]),
                             [pb2k], [repk])
                        oc0 = T_OWN * (h // 2) + 512 * gp
                        ob = 64 * (h % 2)
                        P.op("V", lambda e, ocp=ocp, ob=ob,
                             onorm_sb=onorm_sb, rep_sb=rep_sb:
                             e.tensor_mul(onorm_sb[ob:ob + 64, 0:512],
                                          ocp[0:64, :], rep_sb[0:64, :]),
                             [ock, repk], [onk])
                        osl = slice(oc0, oc0 + 512)
                        oprng = slice(ob, ob + 64)
                        P.op("V", lambda e, osl=osl, oprng=oprng,
                             onorm_sb=onorm_sb: e.tensor_copy(
                                 o_hi[oprng, osl], onorm_sb[oprng, 0:512]),
                             [onk], ["oh"])
                        P.op("V", lambda e, osl=osl, oprng=oprng,
                             onorm_sb=onorm_sb: e.tensor_sub(
                                 o_lo[oprng, osl], onorm_sb[oprng, 0:512],
                                 o_hi[oprng, osl]),
                             [onk, "oh"], ["ol"])

                for g in range(4):
                    qgh, qgl = qgh_sb[g % 2], qgl_sb[g % 2]
                    qghk, qglk = f"qgh{g % 2}", f"qgl{g % 2}"
                    for scr, dst, kk in ((Q_scr_hi, qgh, qghk),
                                         (Q_scr_lo, qgl, qglk)):
                        P.dma("QS", dst[:].rearrange("p (c t) -> p c t",
                                                     t=512),
                              scr[256 * grp:256 * (grp + 1),
                                  512 * g:512 * (g + 1)].rearrange(
                                  "(c p) t -> p c t", p=128),
                              reads=["Q_scr"], writes=[kk])
                    kmax = 8 * g + 8
                    for KK in range(kmax + 1):
                        if KK == 1 and pending_norm:
                            norm_rest(pending_norm.pop(0))
                        jmin = max(0, -(-(KK - 8 * g - 1) // 2))
                        c0 = 128 * jmin
                        for h, phase in [(x, 0) for x in range(4)] + \
                                        [(x, 1) for x in range(4)]:
                            K = KK
                            hc, hb = h // 2, 64 * (h % 2)
                            po = pt[4 + h]
                            pok = pk[4 + h]
                            psb = pt[h]
                            psk = pk[h]
                            emit_scores = (KK < kmax) and phase == 0
                            emit_av = (KK > 0) and phase == 1
                            pre_ps = [pk[hx] for hx in range(4)] \
                                if (emit_scores and h == 0) else []
                            ksl = slice(T * hc + 128 * K,
                                        T * hc + 128 * (K + 1))
                            qsl = slice(512 * hc + c0, 512 * hc + 512)
                            prng = slice(hb, hb + 64)
                            if emit_scores:
                                P.op("T", lambda e, ksl=ksl, qsl=qsl,
                                     prng=prng, psb=psb, c0=c0, qgh=qgh,
                                     kgh_sb=kgh_sb: e.matmul(
                                         psb[:, c0:512], kgh_sb[prng, ksl],
                                         qgh[prng, qsl], start=True,
                                         stop=False),
                                     [kkgh, qghk] + pre_ps, [psk])
                            if emit_scores:
                                P.op("T", lambda e, ksl=ksl, qsl=qsl,
                                     prng=prng, psb=psb, c0=c0, qgl=qgl,
                                     kgh_sb=kgh_sb: e.matmul(
                                         psb[:, c0:512], kgh_sb[prng, ksl],
                                         qgl[prng, qsl], start=False,
                                         stop=False),
                                     [kkgh, qglk], [psk])
                                P.op("T", lambda e, ksl=ksl, qsl=qsl,
                                     prng=prng, psb=psb, c0=c0, qgh=qgh,
                                     kgl_sb=kgl_sb: e.matmul(
                                         psb[:, c0:512], kgl_sb[prng, ksl],
                                         qgh[prng, qsl], start=False,
                                         stop=True),
                                     [kkgl, qghk], [psk])
                                for j in range(jmin, 4):
                                    r = K - 8 * g - 2 * j
                                    if r == 0:
                                        P.op("V", lambda e, j=j, psb=psb:
                                             e.tensor_add(
                                                 psb[:, 128 * j:128 * (j + 1)],
                                                 psb[:, 128 * j:128 * (j + 1)],
                                                 me_sb[:]), [psk, "me"], [psk])
                                    elif r == 1:
                                        P.op("V", lambda e, j=j, psb=psb:
                                             e.tensor_add(
                                                 psb[:, 128 * j:128 * (j + 1)],
                                                 psb[:, 128 * j:128 * (j + 1)],
                                                 mo_sb[:]), [psk, "mo"], [psk])
                                ei = h * 2 + (K % 2)
                                ef = ef_sb[ei]
                                efk = f"ef{ei}"
                                P.op("S", lambda e, psb=psb, ef=ef, c0=c0:
                                     e.activation(ef[:, c0:512],
                                                  psb[:, c0:512],
                                                  AF.Exp, scale=0.125,
                                                  bias=nb_sb[:, 0:1]),
                                     [psk, "nb"], [efk])
                                eh, el = eh_sb[ei], el_sb[ei]
                                ehk, elk = f"eh{ei}", f"el{ei}"
                                P.op("S", lambda e, ef=ef, eh=eh, c0=c0:
                                     e.activation(eh[:, c0:512],
                                                  ef[:, c0:512], AF.Copy),
                                     [efk], [ehk])
                                P.op("V", lambda e, ef=ef, eh=eh, el=el,
                                     c0=c0: e.tensor_sub(
                                         el[:, c0:512], ef[:, c0:512],
                                         eh[:, c0:512]),
                                     [efk, ehk], [elk])
                            if emit_av:
                                Kp = KK - 1
                                jmp = max(0, -(-(Kp - 8 * g - 1) // 2))
                                cp = 128 * jmp
                                ei = h * 2 + (Kp % 2)
                                eh, el = eh_sb[ei], el_sb[ei]
                                ehk, elk = f"eh{ei}", f"el{ei}"
                                if h == 0:
                                    allk = [f"el{hx * 2 + (Kp % 2)}"
                                            for hx in range(4)]
                                    allk += [f"eh{hx * 2 + (Kp % 2)}"
                                             for hx in range(4)]
                                    ehk_all = allk
                                else:
                                    ehk_all = []
                                vsl = slice(260 * Kp + 65 * h,
                                            260 * Kp + 65 * (h + 1))
                                P.op("T", lambda e, vsl=vsl, eh=eh, po=po,
                                     cp=cp, Kp=Kp, vgh_sb=vgh_sb: e.matmul(
                                         po[0:65, cp:512], vgh_sb[:, vsl],
                                         eh[:, cp:512],
                                         start=(Kp == 0), stop=False),
                                     [kvgh, ehk] + ehk_all, [pok])
                                P.op("T", lambda e, vsl=vsl, el=el, po=po,
                                     cp=cp, vgh_sb=vgh_sb: e.matmul(
                                         po[0:65, cp:512], vgh_sb[:, vsl],
                                         el[:, cp:512],
                                         start=False, stop=False),
                                     [kvgh, elk], [pok])
                                P.op("T", lambda e, vsl=vsl, eh=eh, po=po,
                                     cp=cp, Kp=Kp, kmax=kmax, vgl_sb=vgl_sb:
                                     e.matmul(
                                         po[0:65, cp:512], vgl_sb[:, vsl],
                                         eh[:, cp:512],
                                         start=False, stop=(Kp == kmax - 1)),
                                     [kvgl, ehk], [pok])
                    for h in range(4):
                        po = pt[4 + h]
                        pok = pk[4 + h]
                        ocp = ocp_sb[h]
                        ock = f"ocp{h}"
                        P.op("V", lambda e, po=po, ocp=ocp: e.tensor_copy(
                            ocp[0:65, :], po[0:65, 0:512]), [pok], [ock])
                    pending_norm.append(g)
                while pending_norm:
                    norm_rest(pending_norm.pop(0))
                for scr, buf, kk in ((O_scr_hi, o_hi, "oh"),
                                     (O_scr_lo, o_lo, "ol")):
                    P.dma("QA", scr[256 * grp:256 * (grp + 1), :].rearrange(
                        "(c p) t -> p c t", p=128),
                        buf[:].rearrange("p (c t) -> p c t", t=T_OWN),
                        reads=[kk], writes=["O_scr"])

        P.barrier()
        with contextlib.ExitStack() as sc:
            wo_hi = sb(sc, "wo_hi", 128, 8 * 1024, BF16)
            wo_lo = sb(sc, "wo_lo", 128, 8 * 1024, BF16)
            wtmp = sb(sc, "wtmp", 128, 1024)
            ao_sb = [sb(sc, f"ao{i}", 128, 512) for i in range(2)]
            obh_sb = [sb(sc, f"obh{i}", 128, 8 * 512, BF16) for i in range(2)]
            obl_sb = [sb(sc, f"obl{i}", 128, 8 * 512, BF16) for i in range(2)]
            prep_w_hilo(WoT, wo_hi, wo_lo, "woh", "wol", wtmp, "wtmp")
            for g in range(4):
                obh, obl = obh_sb[g % 2], obl_sb[g % 2]
                obhk, oblk = f"obh{g % 2}", f"obl{g % 2}"
                for scr, dst, kk in ((O_scr_hi, obh, obhk),
                                     (O_scr_lo, obl, oblk)):
                    P.dma("QS", dst[:].rearrange("p (c t) -> p c t", t=512),
                          scr[:, 512 * g:512 * (g + 1)].rearrange(
                              "(c p) t -> p c t", p=128),
                          reads=["O_scr"], writes=[kk])
                for oc in range(8):
                    pb = pt[(g * 8 + oc) % 2]
                    pbk = pk[(g * 8 + oc) % 2]
                    for ic in range(8):
                        sl = slice(1024 * ic + 128 * oc,
                                   1024 * ic + 128 * (oc + 1))
                        osl = slice(512 * ic, 512 * (ic + 1))
                        P.op("T", lambda e, sl=sl, osl=osl, pb=pb, ic=ic,
                             obh=obh: e.matmul(
                                 pb[:, :], wo_hi[:, sl], obh[:, osl],
                                 start=(ic == 0), stop=False),
                             ["woh", obhk], [pbk])
                        P.op("T", lambda e, sl=sl, osl=osl, pb=pb, obl=obl:
                             e.matmul(pb[:, :], wo_hi[:, sl], obl[:, osl],
                                      start=False, stop=False),
                             ["woh", oblk], [pbk])
                        P.op("T", lambda e, sl=sl, osl=osl, pb=pb, ic=ic,
                             obh=obh: e.matmul(
                                 pb[:, :], wo_lo[:, sl], obh[:, osl],
                                 start=False, stop=(ic == 7)),
                             ["wol", obhk], [pbk])
                    ao = ao_sb[(g * 8 + oc) % 2]
                    aok = f"ao{(g * 8 + oc) % 2}"
                    P.op("V", lambda e, pb=pb, ao=ao: e.tensor_copy(
                        ao[:], pb[:, :]), [pbk], [aok])
                    P.dma("QA", attn_scr[128 * oc:128 * (oc + 1),
                                         512 * g:512 * (g + 1)], ao[:],
                          reads=[aok], writes=["attn_scr"])


        P.barrier()
        with contextlib.ExitStack() as sc:
            xnf_hi = sb(sc, "xnf_hi", 128, 8 * 512, BF16)
            xnf_lo = sb(sc, "xnf_lo", 128, 8 * 512, BF16)
            hh_hi = sb(sc, "hh_hi", 128, 32 * 512, BF16)
            hh_lo = sb(sc, "hh_lo", 128, 32 * 512, BF16)
            gg_hi = sb(sc, "gg_hi", 128, 32 * 512, BF16)
            gg_lo = sb(sc, "gg_lo", 128, 32 * 512, BF16)
            wb_sb = [sb(sc, f"wb{i}", 128, 32 * 128, BF16) for i in range(2)]
            tf_sb = [sb(sc, f"tf{i}", 128, 512) for i in range(2)]
            atc_sb = [sb(sc, f"atc{i}", 128, 512) for i in range(2)]
            a_sb = sb(sc, "a_sb", 128, 512)
            yp_sb = sb(sc, "yp_sb", 128, 8 * 512)
            xo_sb = sb(sc, "xo_sb", 128, 1024)
            y_sb = sb(sc, "y_sb", 128, 1024)
            sc1_sb = sb(sc, "sc1_sb", 128, 32)
            sc2_sb = sb(sc, "sc2_sb", 128, 32)
            sch1_sb = sb(sc, "sch1_sb", 128, 32)
            sch2_sb = sb(sc, "sch2_sb", 128, 32)
            sc3_sb = sb(sc, "sc3_sb", 128, 8)
            P.dma("QS", sc1_sb[:], s1, writes=["sc1"])
            P.dma("QS", sc2_sb[:], s2, writes=["sc2"])
            P.dma("QS", sch1_sb[:], sh1, writes=["sch1"])
            P.dma("QS", sch2_sb[:], sh2, writes=["sch2"])
            P.dma("QS", sc3_sb[:], s3, writes=["sc3"])

            def split(dst_hi, dst_lo, khi, klo, src_ap, ksrc, off, w):
                P.op("V", lambda e: e.tensor_copy(
                    dst_hi[:, off:off + w], src_ap), [ksrc], [khi])
                P.op("V", lambda e: e.tensor_sub(
                    dst_lo[:, off:off + w], src_ap, dst_hi[:, off:off + w]),
                    [ksrc, khi], [klo])

            def wload(buf, kbuf, wsrc, wsk, oc, nic):
                P.dma("QS", buf[:, 0:128 * nic].rearrange(
                    "p (ic o) -> p ic o", o=128),
                    wsrc[:, 128 * oc:128 * (oc + 1)].rearrange(
                        "(ic p) o -> p ic o", p=128),
                    reads=[wsk], writes=[kbuf])

            for blk in range(NBLK):
                P.dma("QS", xnf_hi[:].rearrange("p (c t) -> p c t", t=512),
                      xnT_scr_hi[:, 512 * blk:512 * (blk + 1)].rearrange(
                          "(c p) t -> p c t", p=128),
                      reads=["xnT_scr"], writes=["xnfh"])
                P.dma("QS", xnf_lo[:].rearrange("p (c t) -> p c t", t=512),
                      xnT_scr_lo[:, 512 * blk:512 * (blk + 1)].rearrange(
                          "(c p) t -> p c t", p=128),
                      reads=["xnT_scr"], writes=["xnfl"])
                for oc in range(32):
                    wb = wb_sb[oc % 2]
                    wbk = f"wb{oc % 2}"
                    wload(wb, wbk, w1b, "w1b", oc, 8)
                    P.dma("QS", wb[:, 1024:2048].rearrange(
                        "p (ic o) -> p ic o", o=128),
                        w2b[:, 128 * oc:128 * (oc + 1)].rearrange(
                            "(ic p) o -> p ic o", p=128),
                        reads=["w2b"], writes=[wbk])
                    pb, pbk = pt[0], pk[0]
                    pb2, pb2k = pt[1], pk[1]
                    for ic in range(8):
                        P.op("T", lambda e, ic=ic, pb=pb, wb=wb: e.matmul(
                            pb[:, :], wb[:, 128 * ic:128 * (ic + 1)],
                            xnf_hi[:, 512 * ic:512 * (ic + 1)],
                            start=(ic == 0), stop=False),
                            [wbk, "xnfh"], [pbk])
                        P.op("T", lambda e, ic=ic, pb=pb, wb=wb: e.matmul(
                            pb[:, :], wb[:, 128 * ic:128 * (ic + 1)],
                            xnf_lo[:, 512 * ic:512 * (ic + 1)],
                            start=False, stop=(ic == 7)),
                            [wbk, "xnfl"], [pbk])
                    for ic in range(8):
                        P.op("T", lambda e, ic=ic, pb2=pb2, wb=wb: e.matmul(
                            pb2[:, :], wb[:, 1024 + 128 * ic:
                                          1024 + 128 * (ic + 1)],
                            xnf_hi[:, 512 * ic:512 * (ic + 1)],
                            start=(ic == 0), stop=False),
                            [wbk, "xnfh"], [pb2k])
                        P.op("T", lambda e, ic=ic, pb2=pb2, wb=wb: e.matmul(
                            pb2[:, :], wb[:, 1024 + 128 * ic:
                                          1024 + 128 * (ic + 1)],
                            xnf_lo[:, 512 * ic:512 * (ic + 1)],
                            start=False, stop=(ic == 7)),
                            [wbk, "xnfl"], [pb2k])
                    P.op("S", lambda e, oc=oc, pb=pb: e.activation(
                        a_sb[:], pb[:, :], AF.Silu,
                        scale=sc1_sb[:, oc:oc + 1]), [pbk, "sc1"], ["a"])
                    tf = tf_sb[oc % 2]
                    tfk = f"tf{oc % 2}"
                    P.op("V", lambda e, oc=oc, pb2=pb2, tf=tf:
                         e.scalar_tensor_tensor(
                             tf[:], pb2[:, :], sc2_sb[:, oc:oc + 1], a_sb[:],
                             ALU.mult, ALU.mult), [pb2k, "sc2", "a"], [tfk])
                    split(hh_hi, hh_lo, "hhh", "hhl", tf[:], tfk,
                          512 * oc, 512)

                def big_stage(wsrc, wsk, scl_sb, sclkey, in_hi, in_lo,
                              kih, kil, out_hi, out_lo, koh, kol):
                    for oc in range(32):
                        wb = wb_sb[oc % 2]
                        wbk = f"wb{oc % 2}"
                        wload(wb, wbk, wsrc, wsk, oc, 32)
                        pb = pt[2 + oc % 2]
                        pbk = pk[2 + oc % 2]
                        for ic in range(32):
                            P.op("T", lambda e, ic=ic, pb=pb, wb=wb,
                                 in_hi=in_hi: e.matmul(
                                     pb[:, :],
                                     wb[:, 128 * ic:128 * (ic + 1)],
                                     in_hi[:, 512 * ic:512 * (ic + 1)],
                                     start=(ic == 0), stop=False),
                                 [wbk, kih], [pbk])
                            P.op("T", lambda e, ic=ic, pb=pb, wb=wb,
                                 in_lo=in_lo: e.matmul(
                                     pb[:, :],
                                     wb[:, 128 * ic:128 * (ic + 1)],
                                     in_lo[:, 512 * ic:512 * (ic + 1)],
                                     start=False, stop=(ic == 31)),
                                 [wbk, kil], [pbk])
                        tf = tf_sb[oc % 2]
                        tfk = f"tf{oc % 2}"
                        P.op("S", lambda e, oc=oc, pb=pb, tf=tf: e.activation(
                            tf[:], pb[:, :], AF.Gelu,
                            scale=scl_sb[:, oc:oc + 1]), [pbk, sclkey], [tfk])
                        split(out_hi, out_lo, koh, kol, tf[:], tfk,
                              512 * oc, 512)

                big_stage(h1b, "h1b", sch1_sb, "sch1", hh_hi, hh_lo,
                          "hhh", "hhl", gg_hi, gg_lo, "ggh", "ggl")
                big_stage(h2b, "h2b", sch2_sb, "sch2", gg_hi, gg_lo,
                          "ggh", "ggl", hh_hi, hh_lo, "hhh", "hhl")

                for oc in range(8):
                    wb = wb_sb[oc % 2]
                    wbk = f"wb{oc % 2}"
                    wload(wb, wbk, w3b, "w3b", oc, 32)
                    atc = atc_sb[oc % 2]
                    atck = f"atc{oc % 2}"
                    P.dma("QS", atc[:],
                          attn_scr[128 * oc:128 * (oc + 1),
                                   512 * blk:512 * (blk + 1)],
                          reads=["attn_scr"], writes=[atck])
                    pb = pt[2 + oc % 2]
                    pbk = pk[2 + oc % 2]
                    for ic in range(32):
                        P.op("T", lambda e, ic=ic, pb=pb, wb=wb: e.matmul(
                            pb[:, :], wb[:, 128 * ic:128 * (ic + 1)],
                            hh_hi[:, 512 * ic:512 * (ic + 1)],
                            start=(ic == 0), stop=False),
                            [wbk, "hhh"], [pbk])
                        P.op("T", lambda e, ic=ic, pb=pb, wb=wb: e.matmul(
                            pb[:, :], wb[:, 128 * ic:128 * (ic + 1)],
                            hh_lo[:, 512 * ic:512 * (ic + 1)],
                            start=False, stop=(ic == 31)),
                            [wbk, "hhl"], [pbk])
                    P.op("V", lambda e, oc=oc, pb=pb, atc=atc:
                         e.scalar_tensor_tensor(
                             yp_sb[:, 512 * oc:512 * (oc + 1)], pb[:, :],
                             sc3_sb[:, oc:oc + 1], atc[:],
                             ALU.mult, ALU.add), [pbk, "sc3", atck], ["yp"])
                for j in range(4):
                    P.dma("QS", xo_sb[:],
                          xq[512 * blk + 128 * j:512 * blk + 128 * (j + 1), :],
                          writes=["xo"])
                    for c in range(8):
                        pb = pt[4 + c // 4]
                        pbk = pk[4 + c // 4]
                        P.op("T", lambda e, c=c, j=j, pb=pb: e.transpose(
                            pb[:, 128 * (c % 4):128 * (c % 4 + 1)],
                            yp_sb[:, 512 * c + 128 * j:512 * c + 128 * (j + 1)],
                            id_sb[:]), ["yp", "id"], [pbk])
                    P.op("V", lambda e: e.tensor_add(
                        y_sb[:, 0:512], pt[4][:, :], xo_sb[:, 0:512]),
                        [pk[4], "xo"], ["ysb"])
                    P.op("V", lambda e: e.tensor_add(
                        y_sb[:, 512:1024], pt[5][:, :], xo_sb[:, 512:1024]),
                        [pk[5], "xo"], ["ysb"])
                    P.dma("QA",
                          y[512 * blk + 128 * j:512 * blk + 128 * (j + 1), :],
                          y_sb[:], reads=["ysb"], writes=["y"])

        P.attach(block)
    return nc


_DEINT = None


def _perm_deint():
    global _DEINT
    if _DEINT is None:
        p = []
        for h in range(H):
            p += [64 * h + 2 * j for j in range(32)]
            p += [64 * h + 2 * j + 1 for j in range(32)]
        _DEINT = np.array(p)
    return _DEINT


def _rope_tables(pos):
    j = np.arange(32, dtype=np.float64)
    freqs = 1.0 / (10000.0 ** (2.0 * j / 64.0))
    ang = pos[None, :].astype(np.float64) * freqs[:, None]
    c32, s32 = np.cos(ang), np.sin(ang)
    Crows = np.concatenate([c32, c32], axis=0)
    Srows = np.concatenate([-s32, s32], axis=0)
    C = np.concatenate([Crows, Crows], axis=0).astype(np.float32)
    S = np.concatenate([Srows, Srows], axis=0).astype(np.float32)
    return np.ascontiguousarray(C), np.ascontiguousarray(S)


def kernel(L0_attn_x, L0_attn_W_qkv, L0_attn_W_out,
           L0_ffn_w1_w, L0_ffn_w1_s, L0_ffn_w2_w, L0_ffn_w2_s,
           L0_ffn_h1_w, L0_ffn_h1_s, L0_ffn_h2_w, L0_ffn_h2_s,
           L0_ffn_w3_w, L0_ffn_w3_s):
    x = np.ascontiguousarray(np.asarray(L0_attn_x, dtype=np.float32))
    Wqkv = np.asarray(L0_attn_W_qkv, dtype=np.float32)
    perm = _perm_deint()
    WqT = np.ascontiguousarray(Wqkv[0:D][perm].T)
    WkT = np.ascontiguousarray(Wqkv[D:2 * D][perm].T)
    WvT = np.ascontiguousarray(Wqkv[2 * D:3 * D].T)
    WoT = np.ascontiguousarray(np.asarray(L0_attn_W_out, dtype=np.float32).T)
    w1T = np.ascontiguousarray(np.asarray(L0_ffn_w1_w, np.float32).T)
    w2T = np.ascontiguousarray(np.asarray(L0_ffn_w2_w, np.float32).T)
    h1T = np.ascontiguousarray(np.asarray(L0_ffn_h1_w, np.float32).T)
    h2T = np.ascontiguousarray(np.asarray(L0_ffn_h2_w, np.float32).T)
    w3T = np.ascontiguousarray(np.asarray(L0_ffn_w3_w, np.float32).T)

    def scl(v, n):
        return np.ascontiguousarray(
            np.asarray(v, np.float32).reshape(n, 128).T)

    s1 = scl(L0_ffn_w1_s, 32)
    s2 = scl(L0_ffn_w2_s, 32)
    sh1 = scl(L0_ffn_h1_s, 32)
    sh2 = scl(L0_ffn_h2_s, 32)
    s3 = scl(L0_ffn_w3_s, 8)

    Cal, Sal = _rope_tables(np.arange(T))
    tri = np.where(np.arange(128)[:, None] <= np.arange(128)[None, :],
                   0.0, NEG).astype(np.float32)
    ident = np.eye(128, dtype=np.float32)
    SW = np.zeros((128, 128), dtype=np.float32)
    SW[np.arange(128), np.arange(128) ^ 32] = 1.0
    ones64 = np.ones((1, 64), dtype=np.float32)

    nc = build_core()
    in_maps = []
    own_rows_all = []
    for c in range(8):
        b, p = c // 2, c % 2
        tiles = np.arange(p, 32, 2)
        own_rows = (tiles[:, None] * 128 + np.arange(128)[None, :]).reshape(-1)
        own_rows_all.append((b, own_rows))
        Cow, Sow = _rope_tables(own_rows)
        if p == 0:
            mask_e, mask_o = tri, np.full((128, 128), NEG, np.float32)
        else:
            mask_e, mask_o = np.zeros((128, 128), np.float32), tri
        in_maps.append({
            "xb": x[b], "xq": np.ascontiguousarray(x[b][own_rows]),
            "WkT": WkT, "WqT": WqT, "WvT": WvT, "WoT": WoT,
            "w1T": w1T, "w2T": w2T, "h1T": h1T, "h2T": h2T, "w3T": w3T,
            "s1": s1, "s2": s2, "sh1": sh1, "sh2": sh2, "s3": s3,
            "Cal": Cal, "Sal": Sal, "Cow": Cow, "Sow": Sow,
            "mask_e": mask_e, "mask_o": mask_o,
            "swp": SW, "ident": ident, "ones64": ones64,
        })
    res = run_bass_kernel_spmd(nc, in_maps, core_ids=list(range(8)))
    out = np.empty((B, T, D), dtype=np.float32)
    for c in range(8):
        b, own_rows = own_rows_all[c]
        out[b][own_rows] = res.results[c]["y"]
    return out


# version 20
# speedup vs baseline: 1.1003x; 1.1003x over previous
import numpy as np
import concourse.bass as bass
import concourse.mybir as mybir
from concourse.bass_utils import run_bass_kernel_spmd

F32 = mybir.dt.float32
BF16 = mybir.dt.bfloat16
AF = mybir.ActivationFunctionType
ALU = mybir.AluOpType
AX = mybir.AxisListType

B, T, D = 4, 4096, 1024
H, HD = 16, 64
DH = 4096
T_OWN = T // 2
NEG = -1.0e30
FFN_TB = 512
NBLK = T_OWN // FFN_TB


class Prog:

    NPOOL = 10

    def __init__(self, nc, sems):
        self.nc = nc
        self.sems = sems
        self.count = {k: 0 for k in sems}
        self.step = {"T": 1, "V": 1, "S": 1}
        self.stream = {"T": "T", "V": "V", "S": "S"}
        for q, st in (("QS", "QS"), ("QA", "S"), ("QG", "QG")):
            for i in range(self.NPOOL):
                self.step[f"{q}{i}"] = 16
                self.stream[f"{q}{i}"] = st
        self.rr = {"QS": 0, "QA": 0, "QG": 0}
        self.last_w = {}
        self.readers = {}
        self.floors = {}
        self.queue = {s: [] for s in ("T", "V", "S", "QS", "QG")}

    def _wait(self, stream, tag, val):
        fl = self.floors.setdefault(stream, {})
        if fl.get(tag, 0) >= val:
            return
        fl[tag] = val
        self.queue[stream].append(("w", tag, val))

    def barrier(self):
        snap = dict(self.count)
        for stream in self.queue:
            for tag, val in snap.items():
                if val and self.stream[tag] != stream:
                    self._wait(stream, tag, val)

    def op(self, tag, fn, reads=(), writes=()):
        stream = self.stream[tag]
        if getattr(self, "serialize_all", False):
            for t2, v2 in self.count.items():
                if v2 and self.stream[t2] != stream:
                    self._wait(stream, t2, v2)
        need = {}
        for k in reads:
            lw = self.last_w.get(k)
            if lw:
                wtag, wval = lw
                if self.stream[wtag] != stream or self.step[wtag] == 16:
                    need[wtag] = max(need.get(wtag, 0), wval)
                elif self.count[wtag] - wval < 8 * self.step[wtag]:
                    need[wtag] = max(need.get(wtag, 0), wval)
        for k in writes:
            lw = self.last_w.get(k)
            if lw:
                wtag, wval = lw
                if self.stream[wtag] != stream or self.step[wtag] == 16:
                    need[wtag] = max(need.get(wtag, 0), wval)
            for rtag, rval in self.readers.get(k, {}).items():
                if self.stream[rtag] != stream or self.step[rtag] == 16:
                    need[rtag] = max(need.get(rtag, 0), rval)
        for t in sorted(need):
            self._wait(stream, t, need[t])
        self.count[tag] += self.step[tag]
        val_after = self.count[tag]
        self.queue[stream].append(("op", tag, fn))
        self.trace_log = getattr(self, "trace_log", [])
        self.trace_log.append((stream, tag, tuple(reads), tuple(writes),
                               dict(need)))
        for k in reads:
            self.readers.setdefault(k, {})[tag] = val_after
        for k in writes:
            self.last_w[k] = (tag, val_after)
            self.readers[k] = {}
        return val_after

    def dma(self, qtag, out, in_, reads=(), writes=()):
        i = self.rr[qtag]
        self.rr[qtag] = (i + 1) % self.NPOOL
        tag = f"{qtag}{i}"
        stream = self.stream[tag]
        self._wait(stream, tag, self.count[tag])
        self.op(tag, lambda e: e.dma_start(out=out, in_=in_), reads, writes)

    def attach(self, block):
        finals = dict(self.count)
        waited = {}
        for q in self.queue.values():
            for kind, tag, x in q:
                if kind == "w":
                    waited.setdefault(tag, set()).add(x)
        for t, v in finals.items():
            if v:
                waited.setdefault(t, set()).add(v)
        cum = {t: 0 for t in self.count}
        pend = {t: 0 for t in self.count}
        incs = {s: [] for s in self.queue}
        for s, q in self.queue.items():
            pass
        for s, q in self.queue.items():
            for kind, tag, x in q:
                if kind != "op":
                    continue
        for s, q in self.queue.items():
            lst = []
            for kind, tag, x in q:
                if kind == "w":
                    lst.append(None)
                    continue
                cum[tag] += self.step[tag]
                pend[tag] += self.step[tag]
                if cum[tag] in waited.get(tag, ()):
                    lst.append((tag, pend[tag]))
                    pend[tag] = 0
                else:
                    lst.append((tag, 0))
            incs[s] = lst

        def mk(stream, final_wait=False):
            def body(e):
                for (kind, tag, x), inc in zip(self.queue[stream],
                                               incs[stream]):
                    if kind == "w":
                        e.wait_ge(self.sems[tag], x)
                    else:
                        ins = x(e)
                        if inc and inc[1]:
                            ins.then_inc(self.sems[inc[0]], inc[1])
                if final_wait:
                    for t in sorted(finals):
                        if finals[t]:
                            e.wait_ge(self.sems[t], finals[t])
            return body
        block.tensor(mk("T"))
        block.vector(mk("V"))
        block.scalar(mk("S"))
        block.gpsimd(mk("QG"))
        block.sync(mk("QS", final_wait=True))


def build_core(dump=False, serialize=False):
    nc = bass.Bass("TRN2", target_bir_lowering=False, debug=False)

    def inp(name, shape):
        return nc.dram_tensor(name, shape, F32, kind="ExternalInput").ap()

    xb = inp("xb", [T, D])
    xq = inp("xq", [T_OWN, D])
    WkT = inp("WkT", [D, D])
    WqT = inp("WqT", [D, D])
    WvT = inp("WvT", [D, D])
    WoT = inp("WoT", [D, D])
    w1T = inp("w1T", [D, DH])
    w2T = inp("w2T", [D, DH])
    h1T = inp("h1T", [DH, DH])
    h2T = inp("h2T", [DH, DH])
    w3T = inp("w3T", [DH, D])
    s1 = inp("s1", [128, 32])
    s2 = inp("s2", [128, 32])
    sh1 = inp("sh1", [128, 32])
    sh2 = inp("sh2", [128, 32])
    s3 = inp("s3", [128, 8])
    Cal = inp("Cal", [128, T])
    Sal = inp("Sal", [128, T])
    Cow = inp("Cow", [128, T_OWN])
    Sow = inp("Sow", [128, T_OWN])
    mask_e = inp("mask_e", [128, 128])
    mask_o = inp("mask_o", [128, 128])
    swp = inp("swp", [128, 128])
    ident = inp("ident", [128, 128])
    ones64 = inp("ones64", [1, 64])

    y = nc.dram_tensor("y", [T_OWN, D], F32, kind="ExternalOutput").ap()

    scr_kind = "ExternalOutput" if dump else "Internal"
    K_scr_hi = nc.dram_tensor("K_scr_hi", [D, T], BF16, kind=scr_kind).ap()
    K_scr_lo = nc.dram_tensor("K_scr_lo", [D, T], BF16, kind=scr_kind).ap()
    Q_scr_hi = nc.dram_tensor("Q_scr_hi", [D, T_OWN], BF16, kind=scr_kind).ap()
    Q_scr_lo = nc.dram_tensor("Q_scr_lo", [D, T_OWN], BF16, kind=scr_kind).ap()
    V_scr_hi = nc.dram_tensor("V_scr_hi", [T, D], BF16, kind=scr_kind).ap()
    V_scr_lo = nc.dram_tensor("V_scr_lo", [T, D], BF16, kind=scr_kind).ap()
    xnT_scr_hi = nc.dram_tensor("xnT_scr_hi", [D, T_OWN], BF16, kind=scr_kind).ap()
    xnT_scr_lo = nc.dram_tensor("xnT_scr_lo", [D, T_OWN], BF16, kind=scr_kind).ap()
    attn_scr = nc.dram_tensor("attn_scr", [D, T_OWN], F32, kind=scr_kind).ap()
    w1b = nc.dram_tensor("w1b", [D, DH], BF16, kind="Internal").ap()
    w2b = nc.dram_tensor("w2b", [D, DH], BF16, kind="Internal").ap()
    h1b = nc.dram_tensor("h1b", [DH, DH], BF16, kind="Internal").ap()
    h2b = nc.dram_tensor("h2b", [DH, DH], BF16, kind="Internal").ap()
    w3b = nc.dram_tensor("w3b", [DH, D], BF16, kind="Internal").ap()
    O_scr_hi = nc.dram_tensor("O_scr_hi", [D, T_OWN], BF16, kind="Internal").ap()
    O_scr_lo = nc.dram_tensor("O_scr_lo", [D, T_OWN], BF16, kind="Internal").ap()
    st_dbg = nc.dram_tensor("st_dbg", [128, 16], F32, kind=scr_kind).ap()
    st_steps = [nc.dram_tensor(f"st_step{i}", [128, 16], F32, kind=scr_kind).ap()
                for i in range(6)]
    h_dbg = nc.dram_tensor("h_dbg", [128, 32 * 512], F32, kind=scr_kind).ap()
    g1_dbg = nc.dram_tensor("g1_dbg", [128, 32 * 512], F32, kind=scr_kind).ap()
    g2_dbg = nc.dram_tensor("g2_dbg", [128, 32 * 512], F32, kind=scr_kind).ap()
    xn_dbg = nc.dram_tensor("xn_dbg", [128, 4096], F32, kind=scr_kind).ap()
    wk_dbg = nc.dram_tensor("wk_dbg", [128, 8192], BF16, kind=scr_kind).ap()
    kraw_dbg = nc.dram_tensor("kraw_dbg", [128, 512], F32, kind=scr_kind).ap()
    wkl2_dbg = nc.dram_tensor("wkl2_dbg", [128, 8192], BF16, kind=scr_kind).ap()
    wkh2_dbg = nc.dram_tensor("wkh2_dbg", [128, 8192], BF16, kind=scr_kind).ap()
    xth_dbg = nc.dram_tensor("xth_dbg", [128, 4096], BF16, kind=scr_kind).ap()
    xtl_dbg = nc.dram_tensor("xtl_dbg", [128, 4096], BF16, kind=scr_kind).ap()
    wkl_dbg = nc.dram_tensor("wkl_dbg", [128, 8192], BF16, kind=scr_kind).ap()

    import contextlib
    with contextlib.ExitStack() as top:
        sems = {}
        for t in ("T", "V", "S"):
            sems[t] = top.enter_context(nc.semaphore(f"sm{t}"))
        for q in ("QS", "QA", "QG"):
            for i in range(Prog.NPOOL):
                sems[f"{q}{i}"] = top.enter_context(
                    nc.semaphore(f"sm{q}{i}"))
        block = top.enter_context(nc.Block())
        P = Prog(nc, sems)
        P.serialize_all = serialize
        globals()['LAST_PROG'] = P

        def sb(stack, name, p, f, dt=F32):
            return stack.enter_context(nc.sbuf_tensor(name, [p, f], dt))

        g_sc = top.enter_context(contextlib.ExitStack())
        swp_sb = sb(g_sc, "swp_sb", 128, 128)
        id_sb = sb(g_sc, "id_sb", 128, 128)
        me_sb = sb(g_sc, "me_sb", 128, 128)
        mo_sb = sb(g_sc, "mo_sb", 128, 128)
        on_sb = sb(g_sc, "on_sb", 1, 64)
        nb_sb = sb(g_sc, "nb_sb", 128, 1)
        P.dma("QS", swp_sb[:], swp, writes=["swp"])
        P.dma("QS", id_sb[:], ident, writes=["id"])
        P.dma("QS", me_sb[:], mask_e, writes=["me"])
        P.dma("QS", mo_sb[:], mask_o, writes=["mo"])
        P.dma("QS", on_sb[:], ones64, writes=["on"])
        P.op("V", lambda e: e.memset(nb_sb[:], -30.0), [], ["nb"])

        pt = [top.enter_context(nc.psum_tensor(f"pt{i}", [128, 512], F32))
              for i in range(8)]
        wcast_jobs = [(w1T, w1b, "w1b"), (w2T, w2b, "w2b"),
                      (h1T, h1b, "h1b"), (h2T, h2b, "h2b"),
                      (w3T, w3b, "w3b")]
        pk = [f"pt{i}" for i in range(8)]

        dump_hooks = {"n": 0, "on": False}

        def stdump(st_sb, stkey):
            if not dump_hooks["on"]:
                return
            i = dump_hooks["n"]
            if i >= len(st_steps):
                return
            dump_hooks["n"] = i + 1
            P.dma("QA", st_steps[i], st_sb[:], reads=[stkey],
                  writes=[f"stddbg{i}"])

        def layernorm(x_sb, xkey, nt, st_sb, stkey, xn_sb, xnkey):
            for j in range(nt):
                P.op("V", lambda e, j=j: e.reduce_sum(
                    st_sb[:, j:j + 1], x_sb[:, 1024 * j:1024 * (j + 1)],
                    axis=AX.X), [xkey], [stkey])
            P.op("V", lambda e: e.tensor_scalar(
                st_sb[:, 0:nt], st_sb[:, 0:nt], 1.0 / D, None, ALU.mult),
                [stkey], [stkey])
            stdump(st_sb, stkey)
            for j in range(nt):
                P.op("S", lambda e, j=j: e.activation(
                    xn_sb[:, 1024 * j:1024 * (j + 1)],
                    x_sb[:, 1024 * j:1024 * (j + 1)], AF.Square,
                    accum_out=st_sb[:, nt + j:nt + j + 1]),
                    [xkey], [xnkey, stkey])
            stdump(st_sb, stkey)
            P.op("V", lambda e: e.tensor_mul(
                st_sb[:, 2 * nt:3 * nt], st_sb[:, 0:nt], st_sb[:, 0:nt]),
                [stkey], [stkey])
            stdump(st_sb, stkey)
            P.op("V", lambda e: e.tensor_scalar(
                st_sb[:, nt:2 * nt], st_sb[:, nt:2 * nt], 1.0 / D, None,
                ALU.mult), [stkey], [stkey])
            P.op("V", lambda e: e.tensor_sub(
                st_sb[:, nt:2 * nt], st_sb[:, nt:2 * nt],
                st_sb[:, 2 * nt:3 * nt]), [stkey], [stkey])
            stdump(st_sb, stkey)
            P.op("V", lambda e: e.tensor_scalar(
                st_sb[:, nt:2 * nt], st_sb[:, nt:2 * nt], 1e-5, None,
                ALU.add), [stkey], [stkey])
            P.op("S", lambda e: e.activation(
                st_sb[:, 2 * nt:3 * nt], st_sb[:, nt:2 * nt], AF.Sqrt),
                [stkey], [stkey])
            P.op("V", lambda e: e.reciprocal(
                st_sb[:, 2 * nt:3 * nt], st_sb[:, 2 * nt:3 * nt]),
                [stkey], [stkey])
            P.op("V", lambda e: e.scalar_tensor_tensor(
                st_sb[:, 3 * nt:4 * nt], st_sb[:, 0:nt], -1.0,
                st_sb[:, 2 * nt:3 * nt], ALU.mult, ALU.mult),
                [stkey], [stkey])
            stdump(st_sb, stkey)
            for j in range(nt):
                P.op("S", lambda e, j=j: e.activation(
                    xn_sb[:, 1024 * j:1024 * (j + 1)],
                    x_sb[:, 1024 * j:1024 * (j + 1)], AF.Identity,
                    bias=st_sb[:, 3 * nt + j:3 * nt + j + 1],
                    scale=st_sb[:, 2 * nt + j:2 * nt + j + 1]),
                    [xkey, stkey], [xnkey])

        def transpose_tiles(xn_sb, xnkey, xnT_sb, xnTkey, nt, pbanks):
            w = 128 * nt
            for c in range(8):
                for j in range(nt):
                    bi = (c * nt + j) % 2
                    pb, pbk = pbanks[bi], pbanks[bi + 2]
                    P.op("T", lambda e, c=c, j=j, pb=pb: e.transpose(
                        pb[:, 0:128], xn_sb[:, 1024 * j + 128 * c:
                                            1024 * j + 128 * (c + 1)], id_sb[:]),
                        [xnkey, "id"], [pbk])
                    P.op("V", lambda e, c=c, j=j, pb=pb: e.tensor_copy(
                        xnT_sb[:, w * c + 128 * j: w * c + 128 * (j + 1)],
                        pb[:, 0:128]), [pbk], [xnTkey])

        def prep_w_hilo(wsrc, whi, wlo, khi, klo, tmp_sb, ktmp, nic=8):
            P.dma("QG", whi[:].rearrange("p (ic o) -> p ic o", o=1024),
                  wsrc.rearrange("(ic p) o -> p ic o", p=128), writes=[khi])
            for ic in range(nic):
                P.dma("QS", tmp_sb[:, 0:1024],
                      wsrc[128 * ic:128 * (ic + 1), :],
                      writes=[ktmp])
                P.op("V", lambda e, ic=ic: e.tensor_sub(
                    wlo[:, 1024 * ic:1024 * (ic + 1)], tmp_sb[:, 0:1024],
                    whi[:, 1024 * ic:1024 * (ic + 1)]),
                    [ktmp, khi], [klo])

        import os
        PROJMODE = os.environ.get("PROJMODE", "3")

        def proj3(pb, pbk, whi, wlo, wkh, wkl, xth, xtl, kxh, kxl, oo):
            if PROJMODE == "1":
                for ic in range(8):
                    sl = slice(1024 * ic + 128 * oo,
                               1024 * ic + 128 * (oo + 1))
                    xsl = slice(512 * ic, 512 * (ic + 1))
                    P.op("T", lambda e, sl=sl, xsl=xsl, pb=pb, whi=whi,
                         xth=xth, ic=ic: e.matmul(
                             pb[:, :], whi[:, sl], xth[:, xsl],
                             start=(ic == 0), stop=(ic == 7)),
                         [wkh, kxh], [pbk])
                return
            for ic in range(8):
                sl = slice(1024 * ic + 128 * oo, 1024 * ic + 128 * (oo + 1))
                xsl = slice(512 * ic, 512 * (ic + 1))
                P.op("T", lambda e, sl=sl, xsl=xsl, pb=pb, whi=whi, xth=xth,
                     ic=ic: e.matmul(pb[:, :], whi[:, sl], xth[:, xsl],
                                     start=(ic == 0), stop=False),
                     [wkh, kxh], [pbk])
                P.op("T", lambda e, sl=sl, xsl=xsl, pb=pb, whi=whi, xtl=xtl:
                     e.matmul(pb[:, :], whi[:, sl], xtl[:, xsl],
                              start=False, stop=False),
                     [wkh, kxl], [pbk])
                P.op("T", lambda e, sl=sl, xsl=xsl, pb=pb, wlo=wlo, xth=xth,
                     ic=ic: e.matmul(pb[:, :], wlo[:, sl], xth[:, xsl],
                                     start=False, stop=(ic == 7)),
                     [wkl, kxh], [pbk])

        def transpose_split(xn_sb, xnkey, xth, xtl, kh, kl, nt, pbanks):
            w = 128 * nt
            for c in range(8):
                for j in range(nt):
                    bi = (c * nt + j) % 2
                    pb, pbk = pbanks[bi], pbanks[bi + 2]
                    P.op("T", lambda e, c=c, j=j, pb=pb: e.transpose(
                        pb[:, 0:128], xn_sb[:, 1024 * j + 128 * c:
                                            1024 * j + 128 * (c + 1)],
                        id_sb[:]), [xnkey, "id"], [pbk])
                    sl = slice(w * c + 128 * j, w * c + 128 * (j + 1))
                    P.op("V", lambda e, sl=sl, pb=pb: e.tensor_copy(
                        xth[:, sl], pb[:, 0:128]), [pbk], [kh])
                    P.op("V", lambda e, sl=sl, pb=pb: e.tensor_sub(
                        xtl[:, sl], pb[:, 0:128], xth[:, sl]),
                        [pbk, kh], [kl])

        def split_to(dst_hi, dst_lo, kh, kl, src, ks, sl):
            P.op("V", lambda e: e.tensor_copy(dst_hi[:, sl], src), [ks], [kh])
            P.op("V", lambda e: e.tensor_sub(dst_lo[:, sl], src,
                                             dst_hi[:, sl]), [ks, kh], [kl])

        with contextlib.ExitStack() as sc:
            wk_hi = sb(sc, "wk_hi", 128, 8 * 1024, BF16)
            wk_lo = sb(sc, "wk_lo", 128, 8 * 1024, BF16)
            wv_hi = sb(sc, "wv_hi", 128, 8 * 1024, BF16)
            wv_lo = sb(sc, "wv_lo", 128, 8 * 1024, BF16)
            cal_sb2 = [sb(sc, f"cal_sb{i}", 128, 512) for i in range(2)]
            sal_sb2 = [sb(sc, f"sal_sb{i}", 128, 512) for i in range(2)]
            x_sb2 = [sb(sc, f"x_sb{i}", 128, 4 * 1024) for i in range(2)]
            xn_sb2 = [sb(sc, f"xn_sb{i}", 128, 4 * 1024) for i in range(2)]
            x_sb = x_sb2[0]
            xth_sb = [sb(sc, f"xth{i}", 128, 8 * 512, BF16) for i in range(2)]
            xtl_sb = [sb(sc, f"xtl{i}", 128, 8 * 512, BF16) for i in range(2)]
            st_sb2 = [sb(sc, f"st_sb{i}", 128, 16) for i in range(2)]
            kraw = [sb(sc, f"kraw{i}", 128, 512) for i in range(2)]
            t1_sb = [sb(sc, f"t1_{i}", 128, 512) for i in range(2)]
            rot_sb = [sb(sc, f"rot{i}", 128, 512) for i in range(2)]
            rh_sb = [sb(sc, f"rh{i}", 128, 512, BF16) for i in range(2)]
            rl_sb = [sb(sc, f"rl{i}", 128, 512, BF16) for i in range(2)]
            vt_sb = [sb(sc, f"vt{i}", 128, 512) for i in range(2)]
            vth_sb = [sb(sc, f"vth{i}", 128, 512, BF16) for i in range(2)]
            vtl_sb = [sb(sc, f"vtl{i}", 128, 512, BF16) for i in range(2)]

            prep_w_hilo(WkT, wk_hi, wk_lo, "wkh", "wkl", x_sb2[0], "x00")
            if dump:
                P.dma("QA", wk_dbg, wk_hi[:], reads=["wkh"], writes=["wkd"])
                P.dma("QA", wkl_dbg, wk_lo[:], reads=["wkl"], writes=["wkld"])
            prep_w_hilo(WvT, wv_hi, wv_lo, "wvh", "wvl", x_sb2[0], "x00")


            for g in range(8):
                x_sb, xn_sb = x_sb2[g % 2], xn_sb2[g % 2]
                kx, kxn = f"x0{g % 2}", f"xn0{g % 2}"
                cal_sb, sal_sb = cal_sb2[g % 2], sal_sb2[g % 2]
                kcal, ksal = f"cal{g % 2}", f"sal{g % 2}"
                P.dma("QS", cal_sb[:], Cal[:, 512 * g:512 * (g + 1)],
                      writes=[kcal])
                P.dma("QS", sal_sb[:], Sal[:, 512 * g:512 * (g + 1)],
                      writes=[ksal])
                P.dma("QS", x_sb[:].rearrange("p (j d) -> p j d", d=1024),
                      xb[512 * g:512 * (g + 1), :].rearrange(
                          "(j p) d -> p j d", p=128),
                      writes=[kx])
                layernorm(x_sb, kx, 4, st_sb2[g % 2], f"st0{g % 2}", xn_sb, kxn)
                xth, xtl = xth_sb[g % 2], xtl_sb[g % 2]
                kxh, kxl = f"xth{g % 2}", f"xtl{g % 2}"
                transpose_split(xn_sb, kxn, xth, xtl, kxh, kxl, 4,
                                (pt[6], pt[7], pk[6], pk[7]))
                def k_tail(oc, g=g, cal_sb=cal_sb, sal_sb=sal_sb,
                           kcal=kcal, ksal=ksal):
                    oo = oc % 8
                    pb, pbk = pt[oc % 2], pk[oc % 2]
                    kr = kraw[oc % 2]
                    krk = f"kraw{oc % 2}"
                    P.op("V", lambda e, pb=pb, kr=kr: e.tensor_copy(
                        kr[:], pb[:, :]), [pbk], [krk])
                    psw = pt[2 + oc % 2]
                    pswk = pk[2 + oc % 2]
                    P.op("T", lambda e, kr=kr, psw=psw: e.matmul(
                        psw[:, :], swp_sb[:], kr[:], start=True,
                        stop=True), [krk, "swp"], [pswk])
                    tl = t1_sb[oc % 2]
                    tlk = f"t1_{oc % 2}"
                    ro = rot_sb[oc % 2]
                    rok = f"rot{oc % 2}"
                    P.op("V", lambda e, kr=kr, tl=tl, cal_sb=cal_sb:
                         e.tensor_mul(tl[:], kr[:], cal_sb[:]),
                         [krk, kcal], [tlk])
                    P.op("V", lambda e, psw=psw, ro=ro, sal_sb=sal_sb:
                         e.tensor_mul(ro[:], psw[:, :], sal_sb[:]),
                         [pswk, ksal], [rok])
                    P.op("V", lambda e, tl=tl, ro=ro: e.tensor_add(
                        ro[:], ro[:], tl[:]), [tlk, rok], [rok])
                    rh, rl = rh_sb[oc % 2], rl_sb[oc % 2]
                    rhk, rlk = f"rh{oc % 2}", f"rl{oc % 2}"
                    P.op("V", lambda e, ro=ro, rh=rh: e.tensor_copy(
                        rh[:], ro[:]), [rok], [rhk])
                    P.op("V", lambda e, ro=ro, rh=rh, rl=rl: e.tensor_sub(
                        rl[:], ro[:], rh[:]), [rok, rhk], [rlk])
                    P.dma("QA", K_scr_hi[128 * oo:128 * (oo + 1),
                                         512 * g:512 * (g + 1)], rh[:],
                          reads=[rhk], writes=["K_scr"])
                    P.dma("QA", K_scr_lo[128 * oo:128 * (oo + 1),
                                         512 * g:512 * (g + 1)], rl[:],
                          reads=[rlk], writes=["K_scr"])

                def v_tail(oc, g=g):
                    oo = oc % 8
                    pb, pbk = pt[oc % 2], pk[oc % 2]
                    vr = kraw[oc % 2]
                    vrk = f"kraw{oc % 2}"
                    P.op("V", lambda e, pb=pb, vr=vr: e.tensor_copy(
                        vr[:], pb[:, :]), [pbk], [vrk])
                    vt = vt_sb[oc % 2]
                    vtk = f"vt{oc % 2}"
                    for j in range(4):
                        pb2 = pt[4 + j % 2]
                        pb2k = pk[4 + j % 2]
                        P.op("T", lambda e, vr=vr, j=j, pb2=pb2:
                             e.transpose(pb2[:, 0:128],
                                         vr[:, 128 * j:128 * (j + 1)],
                                         id_sb[:]),
                             [vrk, "id"], [pb2k])
                        P.op("V", lambda e, pb2=pb2, j=j, vt=vt:
                             e.tensor_copy(vt[:, 128 * j:128 * (j + 1)],
                                           pb2[:, 0:128]), [pb2k], [vtk])
                    vh, vl = vth_sb[oc % 2], vtl_sb[oc % 2]
                    vhk, vlk = f"vth{oc % 2}", f"vtl{oc % 2}"
                    P.op("V", lambda e, vt=vt, vh=vh: e.tensor_copy(
                        vh[:], vt[:]), [vtk], [vhk])
                    P.op("V", lambda e, vt=vt, vh=vh, vl=vl: e.tensor_sub(
                        vl[:], vt[:], vh[:]), [vtk, vhk], [vlk])
                    for scr, vv, kk in ((V_scr_hi, vh, vhk),
                                        (V_scr_lo, vl, vlk)):
                        P.dma("QG",
                              scr[512 * g:512 * (g + 1),
                                  128 * oo:128 * (oo + 1)].rearrange(
                                  "(j p) c -> p j c", p=128),
                              vv[:].rearrange("p (j c) -> p j c", c=128),
                              reads=[kk], writes=["V_scr"])

                for oc in range(17):
                    if oc < 16:
                        pb = pt[oc % 2]
                        pbk = pk[oc % 2]
                        if oc < 8:
                            proj3(pb, pbk, wk_hi, wk_lo, "wkh", "wkl",
                                  xth, xtl, kxh, kxl, oc)
                        else:
                            proj3(pb, pbk, wv_hi, wv_lo, "wvh", "wvl",
                                  xth, xtl, kxh, kxl, oc - 8)
                    if oc > 0:
                        if oc - 1 < 8:
                            k_tail(oc - 1)
                        else:
                            v_tail(oc - 1)

            wq_hi, wq_lo = wk_hi, wk_lo
            cow_sb2 = cal_sb2
            sow_sb2 = sal_sb2
            prep_w_hilo(WqT, wq_hi, wq_lo, "wkh", "wkl", x_sb2[0], "x00")


            for g in range(4):
                x_sb, xn_sb = x_sb2[g % 2], xn_sb2[g % 2]
                kx, kxn = f"x0{g % 2}", f"xn0{g % 2}"
                cow_sb, sow_sb = cow_sb2[g % 2], sow_sb2[g % 2]
                kcow, ksow = f"cal{g % 2}", f"sal{g % 2}"
                P.dma("QS", cow_sb[:], Cow[:, 512 * g:512 * (g + 1)],
                      writes=[kcow])
                P.dma("QS", sow_sb[:], Sow[:, 512 * g:512 * (g + 1)],
                      writes=[ksow])
                P.dma("QS", x_sb[:].rearrange("p (j d) -> p j d", d=1024),
                      xq[512 * g:512 * (g + 1), :].rearrange(
                          "(j p) d -> p j d", p=128),
                      writes=[kx])
                dump_hooks["on"] = dump and g == 0
                layernorm(x_sb, kx, 4, st_sb2[g % 2], f"st0{g % 2}", xn_sb, kxn)
                dump_hooks["on"] = False
                xth, xtl = xth_sb[g % 2], xtl_sb[g % 2]
                kxh, kxl = f"xth{g % 2}", f"xtl{g % 2}"
                transpose_split(xn_sb, kxn, xth, xtl, kxh, kxl, 4,
                                (pt[6], pt[7], pk[6], pk[7]))
                for c in range(8):
                    P.dma("QA", xnT_scr_hi[128 * c:128 * (c + 1),
                                           512 * g:512 * (g + 1)],
                          xth[:, 512 * c:512 * (c + 1)],
                          reads=[kxh], writes=["xnT_scr"])
                    P.dma("QA", xnT_scr_lo[128 * c:128 * (c + 1),
                                           512 * g:512 * (g + 1)],
                          xtl[:, 512 * c:512 * (c + 1)],
                          reads=[kxl], writes=["xnT_scr"])
                def q_tail(oc, g=g, cow_sb=cow_sb, sow_sb=sow_sb,
                           kcow=kcow, ksow=ksow):
                    pb, pbk = pt[oc % 2], pk[oc % 2]
                    qr = kraw[oc % 2]
                    qrk = f"kraw{oc % 2}"
                    P.op("V", lambda e, pb=pb, qr=qr: e.tensor_copy(
                        qr[:], pb[:, :]), [pbk], [qrk])
                    psw = pt[2 + oc % 2]
                    pswk = pk[2 + oc % 2]
                    P.op("T", lambda e, qr=qr, psw=psw: e.matmul(
                        psw[:, :], swp_sb[:], qr[:], start=True, stop=True),
                        [qrk, "swp"], [pswk])
                    tl = t1_sb[oc % 2]
                    tlk = f"t1_{oc % 2}"
                    ro = rot_sb[oc % 2]
                    rok = f"rot{oc % 2}"
                    P.op("V", lambda e, qr=qr, tl=tl, cow_sb=cow_sb:
                         e.tensor_mul(tl[:], qr[:], cow_sb[:]),
                         [qrk, kcow], [tlk])
                    P.op("V", lambda e, psw=psw, ro=ro, sow_sb=sow_sb:
                         e.tensor_mul(ro[:], psw[:, :], sow_sb[:]),
                         [pswk, ksow], [rok])
                    P.op("V", lambda e, tl=tl, ro=ro: e.tensor_add(
                        ro[:], ro[:], tl[:]), [tlk, rok], [rok])
                    rh, rl = rh_sb[oc % 2], rl_sb[oc % 2]
                    rhk, rlk = f"rh{oc % 2}", f"rl{oc % 2}"
                    P.op("V", lambda e, ro=ro, rh=rh: e.tensor_copy(
                        rh[:], ro[:]), [rok], [rhk])
                    P.op("V", lambda e, ro=ro, rh=rh, rl=rl: e.tensor_sub(
                        rl[:], ro[:], rh[:]), [rok, rhk], [rlk])
                    P.dma("QA", Q_scr_hi[128 * oc:128 * (oc + 1),
                                         512 * g:512 * (g + 1)], rh[:],
                          reads=[rhk], writes=["Q_scr"])
                    P.dma("QA", Q_scr_lo[128 * oc:128 * (oc + 1),
                                         512 * g:512 * (g + 1)], rl[:],
                          reads=[rlk], writes=["Q_scr"])

                for oc in range(9):
                    if oc < 8:
                        pb = pt[oc % 2]
                        pbk = pk[oc % 2]
                        proj3(pb, pbk, wq_hi, wq_lo, "wkh", "wkl",
                              xth, xtl, kxh, kxl, oc)
                    if oc > 0:
                        q_tail(oc - 1)

        P.barrier()
        with contextlib.ExitStack() as sc:
            kgh_sb2 = [sb(sc, f"kgh_sb{i}", 128, 2 * T, BF16) for i in range(2)]
            kgl_sb2 = [sb(sc, f"kgl_sb{i}", 128, 2 * T, BF16) for i in range(2)]
            vgh_sb2 = [sb(sc, f"vgh_sb{i}", 128, 32 * 260, BF16) for i in range(2)]
            vgl_sb2 = [sb(sc, f"vgl_sb{i}", 128, 32 * 260, BF16) for i in range(2)]
            qgh_sb = [sb(sc, f"qgh{i}", 128, 2 * 512, BF16) for i in range(2)]
            qgl_sb = [sb(sc, f"qgl{i}", 128, 2 * 512, BF16) for i in range(2)]
            ef_sb = [sb(sc, f"ef{i}", 128, 512) for i in range(8)]
            eh_sb = [sb(sc, f"eh{i}", 128, 512, BF16) for i in range(8)]
            el_sb = [sb(sc, f"el{i}", 128, 512, BF16) for i in range(8)]
            ocp_sb = [sb(sc, f"ocp{i}", 128, 512) for i in range(4)]
            rs_sb2 = [sb(sc, f"rs_sb{i}", 1, 512) for i in range(2)]
            rep_sb2 = [sb(sc, f"rep_sb{i}", 128, 512) for i in range(2)]
            onorm_sb2 = [sb(sc, f"onorm_sb{i}", 128, 512) for i in range(2)]
            o_hi = sb(sc, "o_hi", 128, 2 * T_OWN, BF16)
            o_lo = sb(sc, "o_lo", 128, 2 * T_OWN, BF16)

            def load_grp(grp):
                par = grp % 2
                kgh, kgl = kgh_sb2[par], kgl_sb2[par]
                vgh, vgl = vgh_sb2[par], vgl_sb2[par]
                kkh, kkl = f"kgh{par}", f"kgl{par}"
                vkh, vkl = f"vgh{par}", f"vgl{par}"
                for scr, dst, kk in ((K_scr_hi, kgh, kkh),
                                     (K_scr_lo, kgl, kkl)):
                    P.dma("QG", dst[:].rearrange("p (c t) -> p c t", t=T),
                          scr[256 * grp:256 * (grp + 1), :].rearrange(
                              "(c p) t -> p c t", p=128),
                          reads=["K_scr"], writes=[kk])
                P.op("V", lambda e, vgh=vgh: e.memset(
                    vgh[:].rearrange("p (j h u) -> p j h u", h=4, u=65)
                    [:, :, :, 64:65], 1.0), [], [vkh])
                P.op("V", lambda e, vgl=vgl: e.memset(
                    vgl[:].rearrange("p (j h u) -> p j h u", h=4, u=65)
                    [:, :, :, 64:65], 0.0), [], [vkl])
                for scr, dst, kk in ((V_scr_hi, vgh, vkh),
                                     (V_scr_lo, vgl, vkl)):
                    for j in range(32):
                        P.dma("QG",
                              dst[:, 260 * j:260 * (j + 1)].rearrange(
                                  "p (h u) -> p h u", u=65)[:, :, 0:64],
                              scr[128 * j:128 * (j + 1),
                                  256 * grp:256 * (grp + 1)].rearrange(
                                  "p (h d) -> p h d", d=64),
                              reads=["V_scr"], writes=[kk])

            load_grp(0)
            pending_norm = []
            wjobs = list(wcast_jobs)
            for grp in range(4):
                par = grp % 2
                kgh_sb, kgl_sb = kgh_sb2[par], kgl_sb2[par]
                vgh_sb, vgl_sb = vgh_sb2[par], vgl_sb2[par]
                kkgh, kkgl = f"kgh{par}", f"kgl{par}"
                kvgh, kvgl = f"vgh{par}", f"vgl{par}"
                if grp < 3:
                    load_grp(grp + 1)
                njob = 2 if grp == 0 else (2 if grp == 1 else 1)
                for _ in range(njob):
                    if wjobs:
                        wsrc, wdst, kk = wjobs.pop(0)
                        P.dma("QG", wdst, wsrc, writes=[kk])
                def norm_rest(gp, grp=grp):
                    for h in range(4):
                        ocp = ocp_sb[h]
                        ock = f"ocp{h}"
                        rs_sb, rsk = rs_sb2[h % 2], f"rs{h % 2}"
                        rep_sb, repk = rep_sb2[h % 2], f"rep{h % 2}"
                        onorm_sb, onk = onorm_sb2[h % 2], f"onorm{h % 2}"
                        P.op("V", lambda e, ocp=ocp, rs_sb=rs_sb:
                             e.tensor_scalar(rs_sb[0:1, :], ocp[64:65, :],
                                             1e-30, None, ALU.max),
                             [ock], [rsk])
                        P.op("V", lambda e, rs_sb=rs_sb: e.reciprocal(
                            rs_sb[0:1, :], rs_sb[0:1, :]), [rsk], [rsk])
                        pb2 = pt[h]
                        pb2k = pk[h]
                        P.op("T", lambda e, pb2=pb2, rs_sb=rs_sb: e.matmul(
                            pb2[0:64, :], on_sb[0:1, 0:64], rs_sb[0:1, :],
                            start=True, stop=True), [rsk, "on"], [pb2k])
                        P.op("V", lambda e, pb2=pb2, rep_sb=rep_sb:
                             e.tensor_copy(rep_sb[0:64, :], pb2[0:64, :]),
                             [pb2k], [repk])
                        oc0 = T_OWN * (h // 2) + 512 * gp
                        ob = 64 * (h % 2)
                        P.op("V", lambda e, ocp=ocp, ob=ob,
                             onorm_sb=onorm_sb, rep_sb=rep_sb:
                             e.tensor_mul(onorm_sb[ob:ob + 64, 0:512],
                                          ocp[0:64, :], rep_sb[0:64, :]),
                             [ock, repk], [onk])
                        osl = slice(oc0, oc0 + 512)
                        oprng = slice(ob, ob + 64)
                        P.op("V", lambda e, osl=osl, oprng=oprng,
                             onorm_sb=onorm_sb: e.tensor_copy(
                                 o_hi[oprng, osl], onorm_sb[oprng, 0:512]),
                             [onk], ["oh"])
                        P.op("V", lambda e, osl=osl, oprng=oprng,
                             onorm_sb=onorm_sb: e.tensor_sub(
                                 o_lo[oprng, osl], onorm_sb[oprng, 0:512],
                                 o_hi[oprng, osl]),
                             [onk, "oh"], ["ol"])

                for g in range(4):
                    qgh, qgl = qgh_sb[g % 2], qgl_sb[g % 2]
                    qghk, qglk = f"qgh{g % 2}", f"qgl{g % 2}"
                    for scr, dst, kk in ((Q_scr_hi, qgh, qghk),
                                         (Q_scr_lo, qgl, qglk)):
                        P.dma("QS", dst[:].rearrange("p (c t) -> p c t",
                                                     t=512),
                              scr[256 * grp:256 * (grp + 1),
                                  512 * g:512 * (g + 1)].rearrange(
                                  "(c p) t -> p c t", p=128),
                              reads=["Q_scr"], writes=[kk])
                    kmax = 8 * g + 8
                    for KK in range(kmax + 1):
                        if KK == 1 and pending_norm:
                            norm_rest(pending_norm.pop(0))
                        jmin = max(0, -(-(KK - 8 * g - 1) // 2))
                        c0 = 128 * jmin
                        for h, phase in [(x, 0) for x in range(4)] + \
                                        [(x, 1) for x in range(4)]:
                            K = KK
                            hc, hb = h // 2, 64 * (h % 2)
                            po = pt[4 + h]
                            pok = pk[4 + h]
                            psb = pt[h]
                            psk = pk[h]
                            emit_scores = (KK < kmax) and phase == 0
                            emit_av = (KK > 0) and phase == 1
                            pre_ps = [pk[hx] for hx in range(4)] \
                                if (emit_scores and h == 0) else []
                            ksl = slice(T * hc + 128 * K,
                                        T * hc + 128 * (K + 1))
                            qsl = slice(512 * hc + c0, 512 * hc + 512)
                            prng = slice(hb, hb + 64)
                            if emit_scores:
                                P.op("T", lambda e, ksl=ksl, qsl=qsl,
                                     prng=prng, psb=psb, c0=c0, qgh=qgh,
                                     kgh_sb=kgh_sb: e.matmul(
                                         psb[:, c0:512], kgh_sb[prng, ksl],
                                         qgh[prng, qsl], start=True,
                                         stop=False),
                                     [kkgh, qghk] + pre_ps, [psk])
                            if emit_scores:
                                P.op("T", lambda e, ksl=ksl, qsl=qsl,
                                     prng=prng, psb=psb, c0=c0, qgl=qgl,
                                     kgh_sb=kgh_sb: e.matmul(
                                         psb[:, c0:512], kgh_sb[prng, ksl],
                                         qgl[prng, qsl], start=False,
                                         stop=False),
                                     [kkgh, qglk], [psk])
                                P.op("T", lambda e, ksl=ksl, qsl=qsl,
                                     prng=prng, psb=psb, c0=c0, qgh=qgh,
                                     kgl_sb=kgl_sb: e.matmul(
                                         psb[:, c0:512], kgl_sb[prng, ksl],
                                         qgh[prng, qsl], start=False,
                                         stop=True),
                                     [kkgl, qghk], [psk])
                                for j in range(jmin, 4):
                                    r = K - 8 * g - 2 * j
                                    if r == 0:
                                        P.op("V", lambda e, j=j, psb=psb:
                                             e.tensor_add(
                                                 psb[:, 128 * j:128 * (j + 1)],
                                                 psb[:, 128 * j:128 * (j + 1)],
                                                 me_sb[:]), [psk, "me"], [psk])
                                    elif r == 1:
                                        P.op("V", lambda e, j=j, psb=psb:
                                             e.tensor_add(
                                                 psb[:, 128 * j:128 * (j + 1)],
                                                 psb[:, 128 * j:128 * (j + 1)],
                                                 mo_sb[:]), [psk, "mo"], [psk])
                                ei = h * 2 + (K % 2)
                                ef = ef_sb[ei]
                                efk = f"ef{ei}"
                                P.op("S", lambda e, psb=psb, ef=ef, c0=c0:
                                     e.activation(ef[:, c0:512],
                                                  psb[:, c0:512],
                                                  AF.Exp, scale=0.125,
                                                  bias=nb_sb[:, 0:1]),
                                     [psk, "nb"], [efk])
                                eh, el = eh_sb[ei], el_sb[ei]
                                ehk, elk = f"eh{ei}", f"el{ei}"
                                P.op("S", lambda e, ef=ef, eh=eh, c0=c0:
                                     e.activation(eh[:, c0:512],
                                                  ef[:, c0:512], AF.Copy),
                                     [efk], [ehk])
                                P.op("V", lambda e, ef=ef, eh=eh, el=el,
                                     c0=c0: e.tensor_sub(
                                         el[:, c0:512], ef[:, c0:512],
                                         eh[:, c0:512]),
                                     [efk, ehk], [elk])
                            if emit_av:
                                Kp = KK - 1
                                jmp = max(0, -(-(Kp - 8 * g - 1) // 2))
                                cp = 128 * jmp
                                ei = h * 2 + (Kp % 2)
                                eh, el = eh_sb[ei], el_sb[ei]
                                ehk, elk = f"eh{ei}", f"el{ei}"
                                if h == 0:
                                    allk = [f"el{hx * 2 + (Kp % 2)}"
                                            for hx in range(4)]
                                    allk += [f"eh{hx * 2 + (Kp % 2)}"
                                             for hx in range(4)]
                                    ehk_all = allk
                                else:
                                    ehk_all = []
                                vsl = slice(260 * Kp + 65 * h,
                                            260 * Kp + 65 * (h + 1))
                                P.op("T", lambda e, vsl=vsl, eh=eh, po=po,
                                     cp=cp, Kp=Kp, vgh_sb=vgh_sb: e.matmul(
                                         po[0:65, cp:512], vgh_sb[:, vsl],
                                         eh[:, cp:512],
                                         start=(Kp == 0), stop=False),
                                     [kvgh, ehk] + ehk_all, [pok])
                                P.op("T", lambda e, vsl=vsl, el=el, po=po,
                                     cp=cp, vgh_sb=vgh_sb: e.matmul(
                                         po[0:65, cp:512], vgh_sb[:, vsl],
                                         el[:, cp:512],
                                         start=False, stop=False),
                                     [kvgh, elk], [pok])
                                P.op("T", lambda e, vsl=vsl, eh=eh, po=po,
                                     cp=cp, Kp=Kp, kmax=kmax, vgl_sb=vgl_sb:
                                     e.matmul(
                                         po[0:65, cp:512], vgl_sb[:, vsl],
                                         eh[:, cp:512],
                                         start=False, stop=(Kp == kmax - 1)),
                                     [kvgl, ehk], [pok])
                    for h in range(4):
                        po = pt[4 + h]
                        pok = pk[4 + h]
                        ocp = ocp_sb[h]
                        ock = f"ocp{h}"
                        P.op("V", lambda e, po=po, ocp=ocp: e.tensor_copy(
                            ocp[0:65, :], po[0:65, 0:512]), [pok], [ock])
                    pending_norm.append(g)
                while pending_norm:
                    norm_rest(pending_norm.pop(0))
                for scr, buf, kk in ((O_scr_hi, o_hi, "oh"),
                                     (O_scr_lo, o_lo, "ol")):
                    P.dma("QA", scr[256 * grp:256 * (grp + 1), :].rearrange(
                        "(c p) t -> p c t", p=128),
                        buf[:].rearrange("p (c t) -> p c t", t=T_OWN),
                        reads=[kk], writes=["O_scr"])

        P.barrier()
        with contextlib.ExitStack() as sc:
            wo_hi = sb(sc, "wo_hi", 128, 8 * 1024, BF16)
            wo_lo = sb(sc, "wo_lo", 128, 8 * 1024, BF16)
            wtmp = sb(sc, "wtmp", 128, 1024)
            ao_sb = [sb(sc, f"ao{i}", 128, 512) for i in range(2)]
            obh_sb = [sb(sc, f"obh{i}", 128, 8 * 512, BF16) for i in range(2)]
            obl_sb = [sb(sc, f"obl{i}", 128, 8 * 512, BF16) for i in range(2)]
            prep_w_hilo(WoT, wo_hi, wo_lo, "woh", "wol", wtmp, "wtmp")
            for g in range(4):
                obh, obl = obh_sb[g % 2], obl_sb[g % 2]
                obhk, oblk = f"obh{g % 2}", f"obl{g % 2}"
                for scr, dst, kk in ((O_scr_hi, obh, obhk),
                                     (O_scr_lo, obl, oblk)):
                    P.dma("QS", dst[:].rearrange("p (c t) -> p c t", t=512),
                          scr[:, 512 * g:512 * (g + 1)].rearrange(
                              "(c p) t -> p c t", p=128),
                          reads=["O_scr"], writes=[kk])
                for oc in range(8):
                    pb = pt[(g * 8 + oc) % 2]
                    pbk = pk[(g * 8 + oc) % 2]
                    for ic in range(8):
                        sl = slice(1024 * ic + 128 * oc,
                                   1024 * ic + 128 * (oc + 1))
                        osl = slice(512 * ic, 512 * (ic + 1))
                        P.op("T", lambda e, sl=sl, osl=osl, pb=pb, ic=ic,
                             obh=obh: e.matmul(
                                 pb[:, :], wo_hi[:, sl], obh[:, osl],
                                 start=(ic == 0), stop=False),
                             ["woh", obhk], [pbk])
                        P.op("T", lambda e, sl=sl, osl=osl, pb=pb, obl=obl:
                             e.matmul(pb[:, :], wo_hi[:, sl], obl[:, osl],
                                      start=False, stop=False),
                             ["woh", oblk], [pbk])
                        P.op("T", lambda e, sl=sl, osl=osl, pb=pb, ic=ic,
                             obh=obh: e.matmul(
                                 pb[:, :], wo_lo[:, sl], obh[:, osl],
                                 start=False, stop=(ic == 7)),
                             ["wol", obhk], [pbk])
                    ao = ao_sb[(g * 8 + oc) % 2]
                    aok = f"ao{(g * 8 + oc) % 2}"
                    P.op("V", lambda e, pb=pb, ao=ao: e.tensor_copy(
                        ao[:], pb[:, :]), [pbk], [aok])
                    P.dma("QA", attn_scr[128 * oc:128 * (oc + 1),
                                         512 * g:512 * (g + 1)], ao[:],
                          reads=[aok], writes=["attn_scr"])


        P.barrier()
        with contextlib.ExitStack() as sc:
            xnf_hi = sb(sc, "xnf_hi", 128, 8 * 512, BF16)
            xnf_lo = sb(sc, "xnf_lo", 128, 8 * 512, BF16)
            hh_hi = sb(sc, "hh_hi", 128, 32 * 512, BF16)
            hh_lo = sb(sc, "hh_lo", 128, 32 * 512, BF16)
            gg_hi = sb(sc, "gg_hi", 128, 32 * 512, BF16)
            gg_lo = sb(sc, "gg_lo", 128, 32 * 512, BF16)
            wb_sb = [sb(sc, f"wb{i}", 128, 32 * 128, BF16) for i in range(2)]
            tf_sb = [sb(sc, f"tf{i}", 128, 512) for i in range(2)]
            atc_sb = [sb(sc, f"atc{i}", 128, 512) for i in range(2)]
            a_sb = sb(sc, "a_sb", 128, 512)
            yp_sb = sb(sc, "yp_sb", 128, 8 * 512)
            xo_sb = sb(sc, "xo_sb", 128, 1024)
            y_sb = sb(sc, "y_sb", 128, 1024)
            sc1_sb = sb(sc, "sc1_sb", 128, 32)
            sc2_sb = sb(sc, "sc2_sb", 128, 32)
            sch1_sb = sb(sc, "sch1_sb", 128, 32)
            sch2_sb = sb(sc, "sch2_sb", 128, 32)
            sc3_sb = sb(sc, "sc3_sb", 128, 8)
            P.dma("QS", sc1_sb[:], s1, writes=["sc1"])
            P.dma("QS", sc2_sb[:], s2, writes=["sc2"])
            P.dma("QS", sch1_sb[:], sh1, writes=["sch1"])
            P.dma("QS", sch2_sb[:], sh2, writes=["sch2"])
            P.dma("QS", sc3_sb[:], s3, writes=["sc3"])

            def split(dst_hi, dst_lo, khi, klo, src_ap, ksrc, off, w):
                P.op("V", lambda e: e.tensor_copy(
                    dst_hi[:, off:off + w], src_ap), [ksrc], [khi])
                P.op("V", lambda e: e.tensor_sub(
                    dst_lo[:, off:off + w], src_ap, dst_hi[:, off:off + w]),
                    [ksrc, khi], [klo])

            def wload(buf, kbuf, wsrc, wsk, oc, nic):
                P.dma("QS", buf[:, 0:128 * nic].rearrange(
                    "p (ic o) -> p ic o", o=128),
                    wsrc[:, 128 * oc:128 * (oc + 1)].rearrange(
                        "(ic p) o -> p ic o", p=128),
                    reads=[wsk], writes=[kbuf])

            for blk in range(NBLK):
                P.dma("QS", xnf_hi[:].rearrange("p (c t) -> p c t", t=512),
                      xnT_scr_hi[:, 512 * blk:512 * (blk + 1)].rearrange(
                          "(c p) t -> p c t", p=128),
                      reads=["xnT_scr"], writes=["xnfh"])
                P.dma("QS", xnf_lo[:].rearrange("p (c t) -> p c t", t=512),
                      xnT_scr_lo[:, 512 * blk:512 * (blk + 1)].rearrange(
                          "(c p) t -> p c t", p=128),
                      reads=["xnT_scr"], writes=["xnfl"])
                for oc in range(32):
                    wb = wb_sb[oc % 2]
                    wbk = f"wb{oc % 2}"
                    wload(wb, wbk, w1b, "w1b", oc, 8)
                    P.dma("QS", wb[:, 1024:2048].rearrange(
                        "p (ic o) -> p ic o", o=128),
                        w2b[:, 128 * oc:128 * (oc + 1)].rearrange(
                            "(ic p) o -> p ic o", p=128),
                        reads=["w2b"], writes=[wbk])
                    pb, pbk = pt[0], pk[0]
                    pb2, pb2k = pt[1], pk[1]
                    for ic in range(8):
                        P.op("T", lambda e, ic=ic, pb=pb, wb=wb: e.matmul(
                            pb[:, :], wb[:, 128 * ic:128 * (ic + 1)],
                            xnf_hi[:, 512 * ic:512 * (ic + 1)],
                            start=(ic == 0), stop=False),
                            [wbk, "xnfh"], [pbk])
                        P.op("T", lambda e, ic=ic, pb=pb, wb=wb: e.matmul(
                            pb[:, :], wb[:, 128 * ic:128 * (ic + 1)],
                            xnf_lo[:, 512 * ic:512 * (ic + 1)],
                            start=False, stop=(ic == 7)),
                            [wbk, "xnfl"], [pbk])
                    for ic in range(8):
                        P.op("T", lambda e, ic=ic, pb2=pb2, wb=wb: e.matmul(
                            pb2[:, :], wb[:, 1024 + 128 * ic:
                                          1024 + 128 * (ic + 1)],
                            xnf_hi[:, 512 * ic:512 * (ic + 1)],
                            start=(ic == 0), stop=False),
                            [wbk, "xnfh"], [pb2k])
                        P.op("T", lambda e, ic=ic, pb2=pb2, wb=wb: e.matmul(
                            pb2[:, :], wb[:, 1024 + 128 * ic:
                                          1024 + 128 * (ic + 1)],
                            xnf_lo[:, 512 * ic:512 * (ic + 1)],
                            start=False, stop=(ic == 7)),
                            [wbk, "xnfl"], [pb2k])
                    P.op("S", lambda e, oc=oc, pb=pb: e.activation(
                        a_sb[:], pb[:, :], AF.Silu,
                        scale=sc1_sb[:, oc:oc + 1]), [pbk, "sc1"], ["a"])
                    tf = tf_sb[oc % 2]
                    tfk = f"tf{oc % 2}"
                    P.op("V", lambda e, oc=oc, pb2=pb2, tf=tf:
                         e.scalar_tensor_tensor(
                             tf[:], pb2[:, :], sc2_sb[:, oc:oc + 1], a_sb[:],
                             ALU.mult, ALU.mult), [pb2k, "sc2", "a"], [tfk])
                    split(hh_hi, hh_lo, "hhh", "hhl", tf[:], tfk,
                          512 * oc, 512)

                def big_stage(wsrc, wsk, scl_sb, sclkey, in_hi, in_lo,
                              kih, kil, out_hi, out_lo, koh, kol):
                    for oc in range(32):
                        wb = wb_sb[oc % 2]
                        wbk = f"wb{oc % 2}"
                        wload(wb, wbk, wsrc, wsk, oc, 32)
                        pb = pt[2 + oc % 2]
                        pbk = pk[2 + oc % 2]
                        for ic in range(32):
                            P.op("T", lambda e, ic=ic, pb=pb, wb=wb,
                                 in_hi=in_hi: e.matmul(
                                     pb[:, :],
                                     wb[:, 128 * ic:128 * (ic + 1)],
                                     in_hi[:, 512 * ic:512 * (ic + 1)],
                                     start=(ic == 0), stop=False),
                                 [wbk, kih], [pbk])
                            P.op("T", lambda e, ic=ic, pb=pb, wb=wb,
                                 in_lo=in_lo: e.matmul(
                                     pb[:, :],
                                     wb[:, 128 * ic:128 * (ic + 1)],
                                     in_lo[:, 512 * ic:512 * (ic + 1)],
                                     start=False, stop=(ic == 31)),
                                 [wbk, kil], [pbk])
                        tf = tf_sb[oc % 2]
                        tfk = f"tf{oc % 2}"
                        P.op("S", lambda e, oc=oc, pb=pb, tf=tf: e.activation(
                            tf[:], pb[:, :], AF.Gelu,
                            scale=scl_sb[:, oc:oc + 1]), [pbk, sclkey], [tfk])
                        split(out_hi, out_lo, koh, kol, tf[:], tfk,
                              512 * oc, 512)

                big_stage(h1b, "h1b", sch1_sb, "sch1", hh_hi, hh_lo,
                          "hhh", "hhl", gg_hi, gg_lo, "ggh", "ggl")
                big_stage(h2b, "h2b", sch2_sb, "sch2", gg_hi, gg_lo,
                          "ggh", "ggl", hh_hi, hh_lo, "hhh", "hhl")

                for oc in range(8):
                    wb = wb_sb[oc % 2]
                    wbk = f"wb{oc % 2}"
                    wload(wb, wbk, w3b, "w3b", oc, 32)
                    atc = atc_sb[oc % 2]
                    atck = f"atc{oc % 2}"
                    P.dma("QS", atc[:],
                          attn_scr[128 * oc:128 * (oc + 1),
                                   512 * blk:512 * (blk + 1)],
                          reads=["attn_scr"], writes=[atck])
                    pb = pt[2 + oc % 2]
                    pbk = pk[2 + oc % 2]
                    for ic in range(32):
                        P.op("T", lambda e, ic=ic, pb=pb, wb=wb: e.matmul(
                            pb[:, :], wb[:, 128 * ic:128 * (ic + 1)],
                            hh_hi[:, 512 * ic:512 * (ic + 1)],
                            start=(ic == 0), stop=False),
                            [wbk, "hhh"], [pbk])
                        P.op("T", lambda e, ic=ic, pb=pb, wb=wb: e.matmul(
                            pb[:, :], wb[:, 128 * ic:128 * (ic + 1)],
                            hh_lo[:, 512 * ic:512 * (ic + 1)],
                            start=False, stop=(ic == 31)),
                            [wbk, "hhl"], [pbk])
                    P.op("V", lambda e, oc=oc, pb=pb, atc=atc:
                         e.scalar_tensor_tensor(
                             yp_sb[:, 512 * oc:512 * (oc + 1)], pb[:, :],
                             sc3_sb[:, oc:oc + 1], atc[:],
                             ALU.mult, ALU.add), [pbk, "sc3", atck], ["yp"])
                for j in range(4):
                    P.dma("QS", xo_sb[:],
                          xq[512 * blk + 128 * j:512 * blk + 128 * (j + 1), :],
                          writes=["xo"])
                    for c in range(8):
                        pb = pt[4 + c // 4]
                        pbk = pk[4 + c // 4]
                        P.op("T", lambda e, c=c, j=j, pb=pb: e.transpose(
                            pb[:, 128 * (c % 4):128 * (c % 4 + 1)],
                            yp_sb[:, 512 * c + 128 * j:512 * c + 128 * (j + 1)],
                            id_sb[:]), ["yp", "id"], [pbk])
                    P.op("V", lambda e: e.tensor_add(
                        y_sb[:, 0:512], pt[4][:, :], xo_sb[:, 0:512]),
                        [pk[4], "xo"], ["ysb"])
                    P.op("V", lambda e: e.tensor_add(
                        y_sb[:, 512:1024], pt[5][:, :], xo_sb[:, 512:1024]),
                        [pk[5], "xo"], ["ysb"])
                    P.dma("QA",
                          y[512 * blk + 128 * j:512 * blk + 128 * (j + 1), :],
                          y_sb[:], reads=["ysb"], writes=["y"])

        P.attach(block)
    return nc


_DEINT = None


def _perm_deint():
    global _DEINT
    if _DEINT is None:
        p = []
        for h in range(H):
            p += [64 * h + 2 * j for j in range(32)]
            p += [64 * h + 2 * j + 1 for j in range(32)]
        _DEINT = np.array(p)
    return _DEINT


def _rope_tables(pos):
    j = np.arange(32, dtype=np.float64)
    freqs = 1.0 / (10000.0 ** (2.0 * j / 64.0))
    ang = pos[None, :].astype(np.float64) * freqs[:, None]
    c32, s32 = np.cos(ang), np.sin(ang)
    Crows = np.concatenate([c32, c32], axis=0)
    Srows = np.concatenate([-s32, s32], axis=0)
    C = np.concatenate([Crows, Crows], axis=0).astype(np.float32)
    S = np.concatenate([Srows, Srows], axis=0).astype(np.float32)
    return np.ascontiguousarray(C), np.ascontiguousarray(S)


def kernel(L0_attn_x, L0_attn_W_qkv, L0_attn_W_out,
           L0_ffn_w1_w, L0_ffn_w1_s, L0_ffn_w2_w, L0_ffn_w2_s,
           L0_ffn_h1_w, L0_ffn_h1_s, L0_ffn_h2_w, L0_ffn_h2_s,
           L0_ffn_w3_w, L0_ffn_w3_s):
    x = np.ascontiguousarray(np.asarray(L0_attn_x, dtype=np.float32))
    Wqkv = np.asarray(L0_attn_W_qkv, dtype=np.float32)
    perm = _perm_deint()
    WqT = np.ascontiguousarray(Wqkv[0:D][perm].T)
    WkT = np.ascontiguousarray(Wqkv[D:2 * D][perm].T)
    WvT = np.ascontiguousarray(Wqkv[2 * D:3 * D].T)
    WoT = np.ascontiguousarray(np.asarray(L0_attn_W_out, dtype=np.float32).T)
    w1T = np.ascontiguousarray(np.asarray(L0_ffn_w1_w, np.float32).T)
    w2T = np.ascontiguousarray(np.asarray(L0_ffn_w2_w, np.float32).T)
    h1T = np.ascontiguousarray(np.asarray(L0_ffn_h1_w, np.float32).T)
    h2T = np.ascontiguousarray(np.asarray(L0_ffn_h2_w, np.float32).T)
    w3T = np.ascontiguousarray(np.asarray(L0_ffn_w3_w, np.float32).T)

    def scl(v, n):
        return np.ascontiguousarray(
            np.asarray(v, np.float32).reshape(n, 128).T)

    s1 = scl(L0_ffn_w1_s, 32)
    s2 = scl(L0_ffn_w2_s, 32)
    sh1 = scl(L0_ffn_h1_s, 32)
    sh2 = scl(L0_ffn_h2_s, 32)
    s3 = scl(L0_ffn_w3_s, 8)

    Cal, Sal = _rope_tables(np.arange(T))
    tri = np.where(np.arange(128)[:, None] <= np.arange(128)[None, :],
                   0.0, NEG).astype(np.float32)
    ident = np.eye(128, dtype=np.float32)
    SW = np.zeros((128, 128), dtype=np.float32)
    SW[np.arange(128), np.arange(128) ^ 32] = 1.0
    ones64 = np.ones((1, 64), dtype=np.float32)

    nc = build_core()
    in_maps = []
    own_rows_all = []
    for c in range(8):
        b, p = c // 2, c % 2
        tiles = np.arange(p, 32, 2)
        own_rows = (tiles[:, None] * 128 + np.arange(128)[None, :]).reshape(-1)
        own_rows_all.append((b, own_rows))
        Cow, Sow = _rope_tables(own_rows)
        if p == 0:
            mask_e, mask_o = tri, np.full((128, 128), NEG, np.float32)
        else:
            mask_e, mask_o = np.zeros((128, 128), np.float32), tri
        in_maps.append({
            "xb": x[b], "xq": np.ascontiguousarray(x[b][own_rows]),
            "WkT": WkT, "WqT": WqT, "WvT": WvT, "WoT": WoT,
            "w1T": w1T, "w2T": w2T, "h1T": h1T, "h2T": h2T, "w3T": w3T,
            "s1": s1, "s2": s2, "sh1": sh1, "sh2": sh2, "s3": s3,
            "Cal": Cal, "Sal": Sal, "Cow": Cow, "Sow": Sow,
            "mask_e": mask_e, "mask_o": mask_o,
            "swp": SW, "ident": ident, "ones64": ones64,
        })
    res = run_bass_kernel_spmd(nc, in_maps, core_ids=list(range(8)))
    out = np.empty((B, T, D), dtype=np.float32)
    for c in range(8):
        b, own_rows = own_rows_all[c]
        out[b][own_rows] = res.results[c]["y"]
    return out
